# version 12
# baseline (speedup 1.0000x reference)
import contextlib
import numpy as np
import concourse.bass as bass
import concourse.mybir as mybir
from concourse.bass_utils import run_bass_kernel_spmd

F32 = mybir.dt.float32
BF16 = mybir.dt.bfloat16
AF = mybir.ActivationFunctionType
ALU = mybir.AluOpType
AX = mybir.AxisListType

S = 4096
D = 2048
NH = 16
INTER = 8192
NCORES = 8

SQRT_C = float(128.0 ** -0.25)
R128 = float(128.0 ** -0.5)
R16 = 0.25
R64 = 0.125
EPS_LN = 1e-5
EPS_SM = 1e-35
NEG = -1.0e30

ENGINES = ("tensor", "vector", "scalar", "sync", "gpsimd")


def _hadamard(n):
    h = np.array([[1.0]], dtype=np.float32)
    while h.shape[0] < n:
        h = np.block([[h, h], [h, -h]])
    return np.ascontiguousarray(h, dtype=np.float32)


class Sem:
    def __init__(self, handle):
        self.h = handle
        self.v = 0

    def bump(self, k=1):
        self.v += k
        return (self.h, k, self.v)

    def at(self):
        return (self.h, self.v)


class Em:

    def __init__(self):
        self.ops = {e: [] for e in ENGINES}
        self._pend = {e: [] for e in ENGINES}
        self.n = 0
        self.paranoid = None

    def __call__(self, eng, fn, waits=(), incs=(), dma=False):
        w = self._pend[eng] + [x for x in waits if x is not None]
        if self.paranoid:
            w += [s.at() for s in self.paranoid]
        self._pend[eng] = []
        rec = {"fn": fn, "waits": w, "incs": list(incs), "dma": dma}
        self.ops[eng].append(rec)
        self.n += 1
        return rec

    def gate(self, ph):
        for e in ENGINES:
            if self.ops[e]:
                k = 16 if self.ops[e][-1]["dma"] else 1
                self.ops[e][-1]["incs"].append(ph.bump(k))
        tgt = ph.at()
        for e in ENGINES:
            self._pend[e] = [tgt]

    @staticmethod
    def _replay(recs, E):
        for rec in recs:
            best = {}
            for (sh, v) in rec["waits"]:
                if v > 0 and (sh.num not in best or best[sh.num][1] < v):
                    best[sh.num] = (sh, v)
            for sh, v in best.values():
                E.wait_ge(sh, v)
            ins = rec["fn"](E)
            incs = rec["incs"]
            if incs:
                sh, k, v = incs[0]
                ins.then_inc(sh, k)
                for (sh2, k2, _v2) in incs[1:]:
                    E.wait_ge(sh, v)
                    E.nop().then_inc(sh2, k2)

    def flush(self, nc):
        if not any(self.ops.values()):
            return
        ops = self.ops
        self.ops = {e: [] for e in ENGINES}
        with nc.Block() as blk:
            for e in ENGINES:
                if ops[e]:
                    getattr(blk, e)(lambda E, recs=ops[e]: self._replay(recs, E))


class Rot:

    def __init__(self, slots, sem):
        self.slots = slots
        self.sems = sem if isinstance(sem, list) else [sem] * len(slots)
        self.free_at = [0] * len(slots)
        self.i = 0

    def acquire(self):
        i = self.i % len(self.slots)
        self.i += 1
        v = self.free_at[i]
        return i, self.slots[i], ((self.sems[i].h, v) if v > 0 else None)

    def release(self, i):
        self.free_at[i] = self.sems[i].v


def build(upto=99, debug=()):
    nc = bass.Bass("TRN2", target_bir_lowering=False)

    dt_in = lambda n, s: nc.dram_tensor(n, s, F32, kind="ExternalInput").ap()
    x_in = dt_in("x", [S, D])
    wqkv_t = dt_in("wqkv_t", [24, 16, 128, 256])
    wo_t = dt_in("wo_t", [4, 16, 128, 512])
    wup_t = dt_in("wup_t", [32, 16, 128, 256])
    wgate_t = dt_in("wgate_t", [32, 16, 128, 256])
    wd_t = dt_in("wd_t", [16, 64, 128, 128])
    su_ug_in = dt_in("su_ug_t", [128, 16])
    su_dp_in = dt_in("su_dp_t", [128, 64])
    ws_up_in = dt_in("ws_up_t", [128, 64])
    ws_gate_in = dt_in("ws_gate_t", [128, 64])
    ws_dp_in = dt_in("ws_dp_t", [128, 16])
    h128_in = dt_in("h128", [128, 128])
    h16i8_in = dt_in("h16i8", [128, 128])
    h64i2_in = dt_in("h64i2", [128, 128])
    ident_in = dt_in("ident", [128, 128])
    ones_in = dt_in("ones", [128, 128])
    masks_in = dt_in("masks", [4, 128, 512])
    cvec_in = dt_in("cvec", [128, 8])

    out = nc.dram_tensor("out", [S, D], F32, kind="ExternalOutput").ap()

    def scratch(name, shape):
        kind = "ExternalOutput" if name in debug else "Internal"
        return nc.dram_tensor(name, shape, BF16, kind=kind).ap()

    z1_scr = scratch("z1_scr", [D, S])
    qkv_scr = scratch("qkv_scr", [3 * D, S])
    s_scr = scratch("s_scr", [D, S])
    z2_scr = scratch("z2_scr", [INTER, S])
    r_scr = scratch("r_scr", [16, 8, 128, 512])
    bc_scr = nc.dram_tensor("bc_scr", [2, 512], F32, kind="Internal").ap()

    em = Em()
    with contextlib.ExitStack() as st:
        sb = lambda n, sh, dt: st.enter_context(nc.sbuf_tensor(n, sh, dt))
        ps_ = lambda n, sh, dt: st.enter_context(nc.psum_tensor(n, sh, dt))
        sem = lambda n: Sem(st.enter_context(nc.semaphore(n)))

        h128su_sb = sb("h128su", [128, 2048], BF16)
        h128_sb = sb("h128pm", [128, 128], BF16)
        h16i8_sb = sb("h16i8_c", [128, 128], BF16)
        h64i2_sb = sb("h64i2_c", [128, 128], BF16)
        ident_sb = sb("identb", [128, 128], BF16)
        onesb_sb = sb("onesb", [128, 128], BF16)
        onesf_sb = sb("onesf", [128, 128], F32)
        bias_tab = sb("bias_tab", [128, 16], F32)
        nrm_sb = sb("nrm_sb", [128, 288], F32)
        su_ug_sb = sb("su_ug", [128, 16], F32)
        supeff_sb = sb("supeff", [128, 64], F32)
        wsgate_sb = sb("wsgate", [128, 64], F32)
        wsdp_sb = sb("wsdp", [128, 16], F32)
        sudp_sb = sb("sudp", [128, 64], F32)
        cvec_sb = sb("cvec_c", [128, 8], F32)
        RES = sb("RES", [128, 65536], BF16)
        _ACT_CONSTS = (EPS_LN, -0.5, EPS_SM, SQRT_C)
        for _i, _v in enumerate(_ACT_CONSTS):
            nc.const_aps.register(F32, float(_v), cvec_sb[:, _i:_i + 1])

        psA = ps_("psA", [128, 1024], F32)
        psB = ps_("psB", [128, 1024], F32)
        psC1 = ps_("psC1", [128, 512], F32)
        psC2 = ps_("psC2", [128, 512], F32)
        psD = ps_("psD", [128, 512], F32)
        psT = ps_("psT", [128, 1024], BF16)

        LD = sem("LD")
        WQ = sem("WQ")
        MM = sem("MM")
        T1 = sem("T1")
        EV = sem("EV")
        EV2 = sem("EV2")
        V1 = sem("V1")
        A1 = sem("A1")
        XD = sem("XD")
        XC = sem("XC")
        QS = sem("QS")
        PS = sem("PS")
        PB = sem("PB")
        PX = sem("PX")
        PR = sem("PR")
        PH = sem("PH")
        SD = [sem(f"SD{i}") for i in range(8)]

        ptrot = Rot([psT[:, 0:1024]], V1)

        def dmac(dst, src):
            return lambda E, d=dst, s=src: E.dma_start(out=d, in_=s)

        for dst, src in ((h128_sb[:], h128_in), (h16i8_sb[:], h16i8_in),
                         (h64i2_sb[:], h64i2_in), (ident_sb[:], ident_in),
                         (onesb_sb[:], ones_in)):
            em("gpsimd", dmac(dst, src), incs=[LD.bump(16)], dma=True)
        em("scalar", dmac(onesf_sb[:], ones_in), incs=[LD.bump(16)], dma=True)
        for dst, src in ((su_ug_sb[:], su_ug_in), (supeff_sb[:], ws_up_in),
                         (wsgate_sb[:], ws_gate_in), (wsdp_sb[:], ws_dp_in),
                         (sudp_sb[:], su_dp_in), (cvec_sb[:], cvec_in)):
            em("scalar", dmac(dst, src), incs=[LD.bump(16)], dma=True)
        em("vector", lambda E: E.tensor_mul(supeff_sb[:], supeff_sb[:], sudp_sb[:]),
           waits=[LD.at()], incs=[V1.bump(1)])
        for c in range(16):
            em("vector", lambda E, c=c: E.tensor_scalar_mul(
                h128su_sb[:, c * 128:(c + 1) * 128], h128_sb[:],
                su_ug_sb[:, c:c + 1]),
               incs=[V1.bump(1)] if c == 15 else ())
        em.gate(PH)
        em.flush(nc)

        xn_fm = RES
        if upto >= 1:
            with (nc.sbuf_tensor("xf", [128, 2 * D], F32) as xf,
                  nc.sbuf_tensor("xntm", [128, 2 * D], BF16) as xntm,
                  nc.sbuf_tensor("lnst", [128, 64], F32) as lnst,
                  nc.sbuf_tensor("wqs", [128, 2 * 4096], BF16) as wqs,
                  nc.sbuf_tensor("qst", [128, 2 * 4096], BF16) as qst,
                  nc.sbuf_tensor("sqb", [128, 1024], BF16) as sqb):
                tmrot = Rot([xntm[:, 0:D], xntm[:, D:2 * D]], T1)
                for t in range(32):
                    sl = t % 2
                    xfs = xf[:, sl * D:(sl + 1) * D]
                    ln = lnst[:, sl * 32:sl * 32 + 32]
                    em("sync", lambda E, d=xfs, t=t: E.dma_start(
                        out=d, in_=x_in[t * 128:(t + 1) * 128, :]),
                       waits=[(XC.h, t - 1) if t >= 2 else None],
                       incs=[SD[sl].bump(16)], dma=True)

                    def bn(E, xfs=xfs, ln=ln):
                        for i in range(4):
                            ins = E.bn_stats(ln[:, i * 6:(i + 1) * 6],
                                             xfs[:, i * 512:(i + 1) * 512])
                        return ins
                    em("vector", bn, waits=[SD[sl].at()], incs=[V1.bump(1)])
                    em("vector", lambda E, ln=ln: E.bn_aggr(ln[:, 24:26], ln[:, 0:24]),
                       waits=[V1.at()], incs=[V1.bump(1)])
                    em("scalar", lambda E, ln=ln: E.activation(
                        ln[:, 26:27], ln[:, 25:26], AF.Ln, bias=EPS_LN),
                       waits=[V1.at()], incs=[A1.bump(1)])
                    em("scalar", lambda E, ln=ln: E.activation(
                        ln[:, 27:28], ln[:, 26:27], AF.Exp, scale=-0.5),
                       waits=[A1.at()], incs=[A1.bump(1)])
                    ti, xts, tw = tmrot.acquire()
                    em("vector", lambda E, xfs=xfs, xts=xts, ln=ln: E.tensor_scalar(
                        xts, xfs, ln[:, 24:25], ln[:, 27:28],
                        ALU.subtract, ALU.mult),
                       waits=[A1.at(), tw], incs=[V1.bump(1), XC.bump(1)])
                    xn_ready = V1.v
                    for half in range(2):
                        pi, pslot, pw = ptrot.acquire()

                        def trs(E, xts=xts, half=half, pslot=pslot):
                            for b in range(8):
                                c = half * 8 + b
                                ins = E.transpose(pslot[:, b * 128:(b + 1) * 128],
                                                  xts[:, c * 128:(c + 1) * 128],
                                                  ident_sb[:])
                            return ins
                        em("tensor", trs, waits=[(V1.h, xn_ready), pw],
                           incs=[T1.bump(1)])

                        def tev(E, t=t, half=half, pslot=pslot):
                            for b in range(8):
                                c = half * 8 + b
                                ins = E.tensor_copy(
                                    xn_fm[:, c * 4096 + t * 128:c * 4096 + (t + 1) * 128],
                                    pslot[:, b * 128:(b + 1) * 128])
                            return ins
                        em("vector", tev, waits=[T1.at()], incs=[V1.bump(1)])
                        ptrot.release(pi)
                    tmrot.release(ti)
                xn_done = V1.v
                nrm_v = [0]
                nq_v = [0, 0]
                pend_nm = [None]
                wrot = Rot([wqs[:, 0:4096], wqs[:, 4096:8192]], MM)
                srot = Rot([qst[:, 0:4096], qst[:, 4096:8192]], [SD[2], SD[3]])
                pool4 = Rot([psA[:, 0:512], psA[:, 512:1024],
                             psB[:, 0:512], psB[:, 512:1024]], EV)
                for ob in range(24):
                    wi, wslot, ww = wrot.acquire()
                    em("gpsimd", lambda E, d=wslot, ob=ob: E.dma_start(
                        out=d.rearrange("p (fc o) -> p fc o", fc=16),
                        in_=wqkv_t[ob].transpose([1, 0, 2])),
                       waits=[ww], incs=[SD[wi].bump(16)], dma=True)
                    wq_tgt = SD[wi].at()
                    for oi in range(2):
                        oc = ob * 2 + oi
                        scale = SQRT_C if oc < 32 else 1.0
                        si, estage, sw = srot.acquire()
                        for t in range(8):
                            pi, psx, pw = pool4.acquire()

                            def mms(E, wslot=wslot, oi=oi, t=t, psx=psx):
                                for fc in range(16):
                                    ins = E.matmul(
                                        psx,
                                        wslot[:, fc * 256 + oi * 128:fc * 256 + (oi + 1) * 128],
                                        xn_fm[:, fc * 4096 + t * 512:fc * 4096 + (t + 1) * 512],
                                        start=(fc == 0), stop=(fc == 15))
                                return ins
                            em("tensor", mms, waits=[wq_tgt, pw,
                               (V1.h, xn_done) if (ob == 0 and oi == 0 and t == 0) else None],
                               incs=[T1.bump(1)] + ([MM.bump(1)] if (oi == 1 and t == 7) else []))
                            em("scalar", lambda E, psx=psx, d=estage, t=t, scale=scale:
                               E.activation(d[:, t * 512:(t + 1) * 512], psx,
                                            AF.Copy, scale=scale),
                               waits=[(T1.h, T1.v), sw if t == 0 else None],
                               incs=([] if oc < 32 else [EV.bump(1)]))
                            if pend_nm[0] is not None:
                                qsl2, oc2, t2, ev2 = pend_nm[0]
                                pend_nm[0] = None
                                em("tensor", lambda E, qsl=qsl2: E.matmul(
                                    psC2[0:1, :], onesb_sb[:, 0:1],
                                    sqb[:, qsl * 512:(qsl + 1) * 512],
                                    start=True, stop=True),
                                   waits=[(EV.h, ev2), (V1.h, nrm_v[0])],
                                   incs=[T1.bump(1)])
                                em("vector", lambda E, oc=oc2, t=t2: E.reduce_max(
                                    nrm_sb[0:1, oc * 8 + t:oc * 8 + t + 1],
                                    psC2[0:1, :], axis=AX.X),
                                   waits=[(T1.h, T1.v)], incs=[V1.bump(1)])
                                nrm_v[0] = V1.v
                                nq_v[qsl2] = V1.v
                            if oc < 32:
                                qsl = t % 2
                                em("scalar", lambda E, psx=psx, qsl=qsl:
                                   E.activation(sqb[:, qsl * 512:(qsl + 1) * 512],
                                                psx, AF.Square),
                                   waits=[(V1.h, nq_v[qsl])], incs=[EV.bump(1)])
                                pend_nm[0] = (qsl, oc, t, EV.v)
                            pool4.release(pi)
                        em("sync", lambda E, s=estage, oc=oc: E.dma_start(
                            out=qkv_scr[oc * 128:(oc + 1) * 128, :], in_=s),
                           waits=[(EV.h, EV.v)], incs=[srot.sems[si].bump(16)], dma=True)
                        srot.release(si)
                    wrot.release(wi)
                if pend_nm[0] is not None:
                    qsl2, oc2, t2, ev2 = pend_nm[0]
                    pend_nm[0] = None
                    em("tensor", lambda E, qsl=qsl2: E.matmul(
                        psC2[0:1, :], onesb_sb[:, 0:1],
                        sqb[:, qsl * 512:(qsl + 1) * 512],
                        start=True, stop=True),
                       waits=[(EV.h, ev2), (V1.h, nrm_v[0])], incs=[T1.bump(1)])
                    em("vector", lambda E, oc=oc2, t=t2: E.reduce_max(
                        nrm_sb[0:1, oc * 8 + t:oc * 8 + t + 1],
                        psC2[0:1, :], axis=AX.X),
                       waits=[(T1.h, T1.v)], incs=[V1.bump(1)])
                    nrm_v[0] = V1.v
                norm_done = nrm_v[0]
                for h in range(NH):
                    em("vector", lambda E, h=h: E.reduce_max(
                        nrm_sb[0:1, 256 + (h % 2):256 + (h % 2) + 1],
                        nrm_sb[0:1, h * 8:(h + 1) * 8], axis=AX.X),
                       waits=[(V1.h, norm_done) if h == 0 else V1.at()],
                       incs=[V1.bump(1)])
                    em("vector", lambda E, h=h: E.reduce_max(
                        nrm_sb[0:1, 258:259],
                        nrm_sb[0:1, (16 + h) * 8:(17 + h) * 8], axis=AX.X),
                       waits=[V1.at()], incs=[V1.bump(1)])
                    em("vector", lambda E, h=h: E.tensor_add(
                        nrm_sb[0:1, 259:260], nrm_sb[0:1, 256 + (h % 2):256 + (h % 2) + 1],
                        nrm_sb[0:1, 258:259]),
                       waits=[V1.at()], incs=[V1.bump(1)])
                    em("vector", lambda E, h=h: E.tensor_scalar_mul(
                        nrm_sb[0:1, 260 + h:261 + h],
                        nrm_sb[0:1, 259:260], -0.5025 * SQRT_C * SQRT_C),
                       waits=[V1.at()], incs=[V1.bump(1)])
                    em("tensor", lambda E, h=h: E.matmul(
                        psA[:, h:h + 1] if h % 2 == 0 else psB[:, h:h + 1],
                        onesf_sb[0:1, :],
                        nrm_sb[0:1, 260 + h:261 + h],
                        start=True, stop=True),
                       waits=[V1.at(), (A1.h, A1.v),
                              (EV.h, EV.v) if h < 2 else None],
                       incs=[T1.bump(1)])
                    em("scalar", lambda E, h=h: E.activation(
                        bias_tab[:, h:h + 1],
                        (psA if h % 2 == 0 else psB)[:, h:h + 1], AF.Copy),
                       waits=[(T1.h, T1.v)], incs=[A1.bump(1)])
                bias_done = A1.v
                if "bias_dbg" in debug:
                    bias_dbg = nc.dram_tensor("bias_dbg", [128, 288], F32,
                                              kind="ExternalOutput").ap()
                    em("scalar", lambda E: E.dma_start(
                        out=bias_dbg[:, 0:16], in_=bias_tab[:]),
                       waits=[(A1.h, bias_done)], incs=[PS.bump(16)], dma=True)
                    em("scalar", lambda E: E.dma_start(
                        out=bias_dbg[:, 16:304] if False else bias_dbg[:, 0:288].bitcast(F32),
                        in_=nrm_sb[:]) if False else E.dma_start(
                        out=bias_dbg[:], in_=nrm_sb[:]),
                       waits=[(V1.h, V1.v)], incs=[PS.bump(16)], dma=True)
                pool2 = Rot([psC1[:], psC2[:]], EV2)
                for c in range(16):
                    si, zstage, sw = srot.acquire()
                    for t in range(8):
                        pi, psx, pw = pool2.acquire()
                        em("tensor", lambda E, c=c, t=t, psx=psx: E.matmul(
                            psx, h128su_sb[:, c * 128:(c + 1) * 128],
                            xn_fm[:, c * 4096 + t * 512:c * 4096 + (t + 1) * 512],
                            start=True, stop=True),
                           waits=[pw, (V1.h, norm_done) if (c == 0 and t <= 1) else None],
                           incs=[T1.bump(1)])
                        em("vector", lambda E, psx=psx, d=zstage, t=t:
                           E.tensor_scalar_mul(d[:, t * 512:(t + 1) * 512], psx, R128),
                           waits=[(T1.h, T1.v), sw if t == 0 else None],
                           incs=[EV2.bump(1)])
                        pool2.release(pi)
                    em("sync", lambda E, s=zstage, c=c: E.dma_start(
                        out=z1_scr[c * 128:(c + 1) * 128, :], in_=s),
                       waits=[(EV2.h, EV2.v)], incs=[srot.sems[si].bump(16)], dma=True)
                    srot.release(si)
                em.gate(PH)
                em.flush(nc)

        attnT = RES
        import os
        _TC = bool(os.environ.get("TEST_C"))
        _TD = bool(os.environ.get("TEST_D"))
        if os.environ.get("PARANOID_P2"):
            em.paranoid = [MM, T1, V1, A1, PS, PB, PX, PR]
        if upto >= 3:
            with (nc.sbuf_tensor("qts", [128, 2 * 4096], BF16) as qts,
                  nc.sbuf_tensor("kts", [128, 2 * 4096], BF16) as kts,
                  nc.sbuf_tensor("vfm", [128, 2 * 1024], BF16) as vfm,
                  nc.sbuf_tensor("vtm", [128, 2 * 4096], BF16) as vtm,
                  nc.sbuf_tensor("probs", [128, 3072], BF16) as probs,
                  nc.sbuf_tensor("masks_c", [128, 2048], BF16) as mask_sb,
                  nc.sbuf_tensor("attst", [128, 32], F32) as attst,
                  nc.sbuf_tensor("rssb", [128, 2048], F32) as rssb):
                em("gpsimd", lambda E: E.dma_start(
                    out=mask_sb.rearrange("p (d f) -> p d f", d=4),
                    in_=masks_in.rearrange("d p f -> p d f")),
                   incs=[LD.bump(16)], dma=True)
                mask_ld = LD.at()
                hrot = Rot([0, 1], MM)
                recip_v = [0]
                bc_free = [None, None]
                pr_bank = [0, 0]
                vrot = Rot([vfm[:, 0:1024], vfm[:, 1024:2048]], T1)
                prot = Rot([probs[:, 0:1024], probs[:, 1024:2048],
                            probs[:, 2048:3072]], PX)
                wvrot = Rot([psA, psB], PB)
                for h in range(NH):
                    hi, hs, hw = hrot.acquire()
                    qQ = qts[:, hs * 4096:(hs + 1) * 4096]
                    kK = kts[:, hs * 4096:(hs + 1) * 4096]
                    vT = vtm[:, hs * 4096:(hs + 1) * 4096]
                    em("sync", lambda E, d=qQ, h=h: E.dma_start(
                        out=d, in_=qkv_scr[h * 128:(h + 1) * 128, :]),
                       waits=[hw], incs=[SD[hs].bump(16)], dma=True)
                    em("sync", lambda E, d=kK, h=h: E.dma_start(
                        out=d, in_=qkv_scr[(16 + h) * 128:(17 + h) * 128, :]),
                       incs=[SD[2 + hs].bump(16)], dma=True)
                    qk_tgt = [SD[hs].at(), SD[2 + hs].at()]
                    for quar in range(4):
                        vi, vslot, vw = vrot.acquire()
                        em("sync", lambda E, d=vslot, h=h, quar=quar: E.dma_start(
                            out=d, in_=qkv_scr[(32 + h) * 128:(33 + h) * 128,
                                               quar * 1024:(quar + 1) * 1024]),
                           waits=[vw], incs=[SD[4 + vi].bump(16)], dma=True)
                        v_tgt = SD[4 + vi].at()
                        pi, pslot, pw = ptrot.acquire()

                        def vtr(E, vslot=vslot, pslot=pslot):
                            for b in range(8):
                                ins = E.transpose(pslot[:, b * 128:(b + 1) * 128],
                                                  vslot[:, b * 128:(b + 1) * 128],
                                                  ident_sb[:])
                            return ins
                        em("tensor", vtr, waits=[v_tgt, pw],
                           incs=[T1.bump(1)])
                        vrot.release(vi)

                        def vev(E, vT=vT, quar=quar, pslot=pslot):
                            for b in range(8):
                                kc = quar * 8 + b
                                ins = E.tensor_copy(vT[:, kc * 128:(kc + 1) * 128],
                                                    pslot[:, b * 128:(b + 1) * 128])
                            return ins
                        em("vector", vev, waits=[T1.at()], incs=[V1.bump(1)])
                        ptrot.release(pi)

                    if "vtm_dbg" in debug and h == 1:
                        vtm_dbg = nc.dram_tensor("vtm_dbg", [128, 4096], BF16,
                                                 kind="ExternalOutput").ap()
                        em("sync", lambda E, vT=vT: E.dma_start(out=vtm_dbg, in_=vT),
                           waits=[(V1.h, V1.v)], incs=[PS.bump(16)], dma=True)
                    bias_col = bias_tab[:, h:h + 1]
                    bias_tgt = 0

                    for qg in range(8):
                        nchunk = 4 * qg + 4
                        nwave = nchunk // 2
                        acc = psC1 if qg % 2 == 0 else psD
                        par = qg % 2
                        pend_av = []
                        for w_i in range(nwave):
                            k0 = 2 * w_i
                            ri, reg, rw = wvrot.acquire()
                            for j in range(2):
                                kc = k0 + j
                                em("tensor", lambda E, kc=kc, j=j, reg=reg, kK=kK, qQ=qQ, qg=qg:
                                   E.matmul(reg[:, j * 512:(j + 1) * 512],
                                            kK[:, kc * 128:(kc + 1) * 128],
                                            qQ[:, qg * 512:(qg + 1) * 512],
                                            start=True, stop=True),
                                   waits=[rw if j == 0 else None,
                                          qk_tgt[0] if (qg == 0 and w_i == 0 and j == 0) else None,
                                          qk_tgt[1] if (qg == 0 and w_i == 0 and j == 1) else None],
                                   incs=[T1.bump(1)])
                            mm_tgt = T1.v
                            qi, wslot, qw = prot.acquire()
                            em("scalar", lambda E, reg=reg, wslot=wslot, bias_col=bias_col:
                               E.activation(wslot, reg[:], AF.Exp, bias=bias_col),
                               waits=[(T1.h, mm_tgt), qw],
                               incs=[PB.bump(1)])
                            wvrot.release(ri)
                            mask_tgt = None
                            for j in range(2):
                                dd = (k0 + j) - 4 * qg
                                if dd >= 0:
                                    em("vector", lambda E, j=j, wslot=wslot, dd=dd:
                                       E.tensor_mul(wslot[:, j * 512:(j + 1) * 512],
                                                    wslot[:, j * 512:(j + 1) * 512],
                                                    mask_sb[:, dd * 512:(dd + 1) * 512]),
                                       waits=[PB.at(), mask_ld], incs=[V1.bump(1)])
                                    mask_tgt = (V1.h, V1.v)

                            def avmm(E, k0=k0, wslot=wslot, vT=vT, nchunk=nchunk, acc=acc):
                                for j in range(2):
                                    kc = k0 + j
                                    E.matmul(acc[:], vT[:, kc * 128:(kc + 1) * 128],
                                             wslot[:, j * 512:(j + 1) * 512],
                                             start=(kc == 0), stop=(kc == nchunk - 1))
                                for j in range(2):
                                    kc = k0 + j
                                    ins = E.matmul(psC2[0:1, :], onesb_sb[:, 0:1],
                                                   wslot[:, j * 512:(j + 1) * 512],
                                                   start=(kc == 0), stop=(kc == nchunk - 1))
                                return ins
                            this_av = dict(
                                fn=avmm,
                                waits=[PB.at(), mask_tgt,
                                       (PR.h, pr_bank[par]) if w_i == 0 else None,
                                       (V1.h, recip_v[0]) if w_i == 0 else None])
                            pend_av.append(this_av)
                            if len(pend_av) > 2:
                                p2_ = pend_av.pop(0)
                                em("tensor", p2_["fn"], waits=p2_["waits"],
                                   incs=[PX.bump(1)])
                            prot.release(qi)
                            prot.free_at[qi] = PX.v + len(pend_av)
                        for p2_ in pend_av:
                            em("tensor", p2_["fn"], waits=p2_["waits"],
                               incs=[PX.bump(1)])
                        av_tgt = PX.v
                        em("vector", lambda E, par=par: E.reciprocal(
                            rssb[0:1, 1024 + par * 512:1536 + par * 512],
                            psC2[0:1, :]),
                           waits=[(PX.h, av_tgt)], incs=[V1.bump(1)])
                        recip_v[0] = V1.v
                        em("gpsimd", lambda E, par=par: E.dma_start(
                            out=bc_scr[par:par + 1, :],
                            in_=rssb[0:1, 1024 + par * 512:1536 + par * 512]),
                           waits=[(V1.h, recip_v[0]), bc_free[par]],
                           incs=[SD[6].bump(16)], dma=True)
                        d1 = SD[6].at()
                        em("gpsimd", lambda E, par=par: E.dma_start(
                            out=rssb[:, par * 512:(par + 1) * 512],
                            in_=bc_scr[par:par + 1, :].broadcast_to([128, 512])),
                           waits=[d1], incs=[SD[7].bump(16)], dma=True)
                        d2 = SD[7].at()
                        em("vector", lambda E, h=h, qg=qg, par=par, acc=acc:
                           E.tensor_mul(
                               attnT[:, h * 4096 + qg * 512:h * 4096 + (qg + 1) * 512],
                               acc[:], rssb[:, par * 512:(par + 1) * 512]),
                           waits=[d2], incs=[V1.bump(1), PR.bump(1)])
                        bc_free[par] = (V1.h, V1.v)
                        pr_bank[par] = PR.v
                    em("tensor", lambda E: E.matmul(
                        psC2[0:1, 0:1], onesb_sb[0:1, 0:1], onesb_sb[0:1, 0:1],
                        start=True, stop=True),
                       waits=[(PX.h, PX.v), (V1.h, recip_v[0])],
                       incs=[T1.bump(1), MM.bump(1)])
                    hrot.release(hi)
                em.paranoid = None
                if "attnT_dbg" in debug:
                    att_dbg = nc.dram_tensor("attnT_dbg", [D, S], BF16,
                                             kind="ExternalOutput").ap()
                    for cc in range(16):
                        em("sync", lambda E, cc=cc: E.dma_start(
                            out=att_dbg[cc * 128:(cc + 1) * 128, :],
                            in_=attnT[:, cc * 4096:(cc + 1) * 4096]),
                           waits=[(PR.h, PR.v) if cc == 0 else None],
                           incs=[QS.bump(16)], dma=True)
                em.gate(PH)
                em.flush(nc)

        if upto >= 4:
            with (nc.sbuf_tensor("wos", [128, 2 * 8192], BF16) as wos,
                  nc.sbuf_tensor("sst", [128, 2 * 4096], BF16) as sst):
                wrot = Rot([wos[:, 0:8192], wos[:, 8192:16384]], MM)
                srot = Rot([sst[:, 0:4096], sst[:, 4096:8192]], [SD[2], SD[3]])
                pool4 = Rot([psA[:, 0:512], psA[:, 512:1024],
                             psB[:, 0:512], psB[:, 512:1024]], EV)
                for ob in range(4):
                    wi, wslot, ww = wrot.acquire()
                    em("gpsimd", lambda E, d=wslot, ob=ob: E.dma_start(
                        out=d.rearrange("p (fc o) -> p fc o", fc=16),
                        in_=wo_t[ob].transpose([1, 0, 2])),
                       waits=[ww], incs=[SD[wi].bump(16)], dma=True)
                    wq_tgt = SD[wi].at()
                    for oi in range(4):
                        oc = ob * 4 + oi
                        si, estage, sw = srot.acquire()
                        for t in range(8):
                            pi, psx, pw = pool4.acquire()

                            def mms(E, wslot=wslot, oi=oi, t=t, psx=psx):
                                for fc in range(16):
                                    ins = E.matmul(
                                        psx,
                                        wslot[:, fc * 512 + oi * 128:fc * 512 + (oi + 1) * 128],
                                        attnT[:, fc * 4096 + t * 512:fc * 4096 + (t + 1) * 512],
                                        start=(fc == 0), stop=(fc == 15))
                                return ins
                            em("tensor", mms, waits=[wq_tgt, pw],
                               incs=[T1.bump(1)] + ([MM.bump(1)] if (oi == 3 and t == 7) else []))
                            em("scalar", lambda E, psx=psx, d=estage, t=t:
                               E.activation(d[:, t * 512:(t + 1) * 512], psx, AF.Copy),
                               waits=[(T1.h, T1.v), sw if t == 0 else None],
                               incs=[EV.bump(1)])
                            pool4.release(pi)
                        em("sync", lambda E, s=estage, oc=oc: E.dma_start(
                            out=s_scr[oc * 128:(oc + 1) * 128, :], in_=s),
                           waits=[(EV.h, EV.v)], incs=[srot.sems[si].bump(16)], dma=True)
                        srot.release(si)
                    wrot.release(wi)
                em.gate(PH)
                em.flush(nc)

        x1 = RES
        if upto >= 5:
            with nc.sbuf_tensor("zld", [128, 2 * 4096], BF16) as zld:
                lrot = Rot([zld[:, 0:4096], zld[:, 4096:8192]], T1)
                pool2 = Rot([psC1[:], psC2[:]], EV2)
                z1v = z1_scr.rearrange("(p c) t -> c p t", c=16)
                for c in range(16):
                    li, lslot, lw = lrot.acquire()
                    em("sync", lambda E, d=lslot, c=c: E.dma_start(out=d, in_=z1v[c]),
                       waits=[lw], incs=[SD[li].bump(16)], dma=True)
                    ld_tgt = SD[li].at()
                    for t in range(8):
                        pi, psx, pw = pool2.acquire()
                        em("tensor", lambda E, lslot=lslot, t=t, psx=psx: E.matmul(
                            psx, h16i8_sb[:], lslot[:, t * 512:(t + 1) * 512],
                            start=True, stop=True),
                           waits=[ld_tgt, pw], incs=[T1.bump(1)])
                        em("vector", lambda E, psx=psx, c=c, t=t: E.tensor_scalar_mul(
                            x1[:, c * 4096 + t * 512:c * 4096 + (t + 1) * 512], psx, R16),
                           waits=[(T1.h, T1.v)], incs=[EV2.bump(1)])
                        pool2.release(pi)
                    lrot.release(li)
                em.gate(PH)
                em.flush(nc)

        if upto >= 6:
            with (nc.sbuf_tensor("wub", [128, 2 * 4096], BF16) as wub,
                  nc.sbuf_tensor("wgb", [128, 2 * 4096], BF16) as wgb,
                  nc.sbuf_tensor("ug32", [128, 2048], F32) as ug32,
                  nc.sbuf_tensor("ustg", [128, 2 * 512], BF16) as ustg,
                  nc.sbuf_tensor("zst", [128, 2 * 4096], BF16) as zst):
                wurot = Rot([wub[:, 0:4096], wub[:, 4096:8192]], MM)
                wgrot = Rot([wgb[:, 0:4096], wgb[:, 4096:8192]], MM)
                zrot = Rot([zst[:, 0:4096], zst[:, 4096:8192]], [SD[4], SD[5]])
                uprot = Rot([psA[:, 0:512], psA[:, 512:1024]], EV)
                gprot = Rot([psB[:, 0:512], psB[:, 512:1024]], EV)
                zprot = Rot([psC1[:], psC2[:]], EV2)
                ugrot = Rot([ug32[:, 0:1024], ug32[:, 1024:2048]], V1)
                usrot = Rot([ustg[:, 0:512], ustg[:, 512:1024]], T1)
                u_dbg = None
                if "u_dbg" in debug:
                    u_dbg = nc.dram_tensor("u_dbg", [INTER, S], BF16,
                                           kind="ExternalOutput").ap()
                pend_z = None
                for jb in range(32):
                    ui, wu, uw = wurot.acquire()
                    gi, wg, gw = wgrot.acquire()
                    em("gpsimd", lambda E, d=wu, jb=jb: E.dma_start(
                        out=d.rearrange("p (c j) -> p c j", c=16),
                        in_=wup_t[jb].transpose([1, 0, 2])),
                       waits=[uw], incs=[SD[ui].bump(16)], dma=True)
                    em("gpsimd", lambda E, d=wg, jb=jb: E.dma_start(
                        out=d.rearrange("p (c j) -> p c j", c=16),
                        in_=wgate_t[jb].transpose([1, 0, 2])),
                       waits=[gw], incs=[SD[2 + gi].bump(16)], dma=True)
                    wq_tgt = [SD[ui].at(), SD[2 + gi].at()]
                    for ji in range(2):
                        jc = jb * 2 + ji
                        zi, zstage, zw = zrot.acquire()
                        for t in range(8):
                            pu_i, psu, puw = uprot.acquire()
                            pg_i, psg, pgw = gprot.acquire()

                            def ummd(E, wu=wu, ji=ji, t=t, psu=psu):
                                for c in range(16):
                                    ins = E.matmul(
                                        psu,
                                        wu[:, c * 256 + ji * 128:c * 256 + (ji + 1) * 128],
                                        x1[:, c * 4096 + t * 512:c * 4096 + (t + 1) * 512],
                                        start=(c == 0), stop=(c == 15))
                                return ins
                            em("tensor", ummd, waits=wq_tgt + [puw],
                               incs=[T1.bump(1)])
                            up_tgt = T1.v

                            def gmmd(E, wg=wg, ji=ji, t=t, psg=psg):
                                for c in range(16):
                                    ins = E.matmul(
                                        psg,
                                        wg[:, c * 256 + ji * 128:c * 256 + (ji + 1) * 128],
                                        x1[:, c * 4096 + t * 512:c * 4096 + (t + 1) * 512],
                                        start=(c == 0), stop=(c == 15))
                                return ins
                            em("tensor", gmmd, waits=[pgw], incs=[T1.bump(1)])
                            gt_tgt = T1.v

                            si2, ugslot, sw2 = ugrot.acquire()
                            upf = ugslot[:, 0:512]
                            gaf = ugslot[:, 512:1024]
                            em("scalar", lambda E, psu=psu, d=upf, jc=jc:
                               E.activation(d, psu, AF.Copy,
                                            scale=supeff_sb[:, jc:jc + 1]),
                               waits=[(T1.h, up_tgt), sw2], incs=[EV.bump(1)])
                            uprot.release(pu_i)
                            em("scalar", lambda E, psg=psg, d=gaf, jc=jc:
                               E.activation(d, psg, AF.Silu,
                                            scale=wsgate_sb[:, jc:jc + 1]),
                               waits=[(T1.h, gt_tgt)], incs=[EV.bump(1)])
                            gprot.release(pg_i)
                            usi, ut, usw = usrot.acquire()
                            em("vector", lambda E, d=ut, a=upf, b=gaf:
                               E.tensor_mul(d, a, b),
                               waits=[(EV.h, EV.v), usw], incs=[V1.bump(1)])
                            ugrot.release(si2)
                            u_tgt = V1.v
                            if u_dbg is not None:
                                em("scalar", lambda E, s=ut, jc=jc, t=t: E.dma_start(
                                    out=u_dbg[jc * 128:(jc + 1) * 128,
                                              t * 512:(t + 1) * 512], in_=s),
                                   waits=[(V1.h, u_tgt)], incs=[PS.bump(16)], dma=True)
                            pz_i, psz, pzw = zprot.acquire()

                            def do_z(E, ps=psz, u=ut):
                                return E.matmul(ps, h128_sb[:], u,
                                                start=True, stop=True)
                            this_z = dict(fn=do_z, waits=[(V1.h, u_tgt), pzw],
                                          t=t, psz=psz, usi=usi, pz_i=pz_i,
                                          zstage=zstage, zw=zw, jc=jc, zi=zi)
                            if pend_z is not None:
                                p = pend_z
                                em("tensor", p["fn"], waits=p["waits"],
                                   incs=[T1.bump(1)] + ([MM.bump(1)] if p["t"] == 7 else []))
                                usrot.release(p["usi"])
                                em("vector", lambda E, ps=p["psz"], d=p["zstage"], t=p["t"]:
                                   E.tensor_scalar_mul(d[:, t * 512:(t + 1) * 512], ps, R128),
                                   waits=[(T1.h, T1.v),
                                          p["zw"] if p["t"] == 0 else None],
                                   incs=[EV2.bump(1)])
                                zprot.release(p["pz_i"])
                                if p["t"] == 7:
                                    em("sync", lambda E, s=p["zstage"], jc=p["jc"]:
                                       E.dma_start(
                                           out=z2_scr[jc * 128:(jc + 1) * 128, :],
                                           in_=s),
                                       waits=[(EV2.h, EV2.v)],
                                       incs=[zrot.sems[p["zi"]].bump(16)], dma=True)
                                    zrot.release(p["zi"])
                            pend_z = this_z
                    wurot.release(ui)
                    wurot.free_at[ui] = MM.v + 1
                    wgrot.release(gi)
                    wgrot.free_at[gi] = MM.v + 1
                p = pend_z
                em("tensor", p["fn"], waits=p["waits"],
                   incs=[T1.bump(1), MM.bump(1)])
                usrot.release(p["usi"])
                em("vector", lambda E, ps=p["psz"], d=p["zstage"], t=p["t"]:
                   E.tensor_scalar_mul(d[:, t * 512:(t + 1) * 512], ps, R128),
                   waits=[(T1.h, T1.v)], incs=[EV2.bump(1)])
                zprot.release(p["pz_i"])
                em("sync", lambda E, s=p["zstage"], jc=p["jc"]: E.dma_start(
                    out=z2_scr[jc * 128:(jc + 1) * 128, :], in_=s),
                   waits=[(EV2.h, EV2.v)], incs=[zrot.sems[p["zi"]].bump(16)], dma=True)
                zrot.release(p["zi"])
                em.gate(PH)
                em.flush(nc)

        x2 = RES
        if upto >= 7:
            with (nc.sbuf_tensor("z2ld", [128, 2 * 1024], BF16) as z2ld,
                  nc.sbuf_tensor("wdb", [128, 2 * 8192], BF16) as wdb,
                  nc.sbuf_tensor("gsl", [128, 512], F32) as gsl,
                  nc.sbuf_tensor("ssl", [128, 2 * 1024], BF16) as ssl,
                  nc.sbuf_tensor("rsl", [128, 2 * 512], BF16) as rsl,
                  nc.sbuf_tensor("xw", [128, 2048], F32) as xw,
                  nc.sbuf_tensor("rbuf", [128, 8192], BF16) as rbuf,
                  nc.sbuf_tensor("ost", [128, 1024], F32) as ost):
                z2v = z2_scr.rearrange("(p c) t -> c p t", c=64)
                lrot = Rot([z2ld[:, 0:1024], z2ld[:, 1024:2048]], T1)
                pool2 = Rot([psC1[:], psC2[:]], EV2)
                wdrot = Rot([wdb[:, 0:8192], wdb[:, 8192:16384]], MM)
                dpool = Rot([psA[:, 0:512], psA[:, 512:1024],
                             psB[:, 0:512], psB[:, 512:1024]], EV)
                grot = Rot([gsl[:, 0:512]], V1)
                slrot = Rot([ssl[:, 0:1024], ssl[:, 1024:2048]], V1)
                rrot = Rot([rsl[:, 0:512], rsl[:, 512:1024]], [SD[6], SD[7]])
                down_done = MM.v
                fin_state = {"rb": 0, "xw": 0, "ost": 0}

                def emit_final(tb5):
                    rlw = [(SD[6].h, SD[6].v), (SD[7].h, SD[7].v)]
                    for o2c in range(16):
                        em("gpsimd", lambda E, o2c=o2c, tb5=tb5: E.dma_start(
                            out=rbuf[:, o2c * 512:(o2c + 1) * 512],
                            in_=r_scr[o2c, tb5]),
                           waits=rlw + [(T1.h, fin_state["rb"]) if o2c == 0 else None],
                           incs=[XD.bump(16)], dma=True)
                    r_tgt = XD.at()
                    for sub in range(4):
                        tb = tb5 * 4 + sub
                        em("sync", lambda E, tb=tb: E.dma_start(
                            out=xw[:], in_=x_in[tb * 128:(tb + 1) * 128, :]),
                           waits=[(V1.h, fin_state["xw"])],
                           incs=[PS.bump(16)], dma=True)
                        x_tgt = PS.at()
                        ow = (QS.h, fin_state["ost"])
                        for g in range(2):
                            pi, pslot, pw = ptrot.acquire()

                            def trp(E, sub=sub, g=g, pslot=pslot):
                                for b in range(8):
                                    o2c = g * 8 + b
                                    ins = E.transpose(
                                        pslot[:, b * 128:(b + 1) * 128],
                                        rbuf[:, o2c * 512 + sub * 128:o2c * 512 + (sub + 1) * 128],
                                        ident_sb[:])
                                return ins
                            em("tensor", trp, waits=[r_tgt, pw], incs=[T1.bump(1)])
                            fin_state["rb"] = T1.v

                            def oadd(E, g=g, pslot=pslot):
                                for b in range(8):
                                    o2c = g * 8 + b
                                    ins = E.tensor_add(
                                        ost[:, b * 128:(b + 1) * 128],
                                        pslot[:, b * 128:(b + 1) * 128],
                                        xw[:, o2c * 128:(o2c + 1) * 128])
                                return ins
                            em("vector", oadd,
                               waits=[T1.at(), x_tgt if g == 0 else None, ow],
                               incs=[V1.bump(1)])
                            ptrot.release(pi)
                            em("sync", lambda E, tb=tb, g=g: E.dma_start(
                                out=out[tb * 128:(tb + 1) * 128,
                                        g * 1024:(g + 1) * 1024], in_=ost[:]),
                               waits=[(V1.h, V1.v)], incs=[QS.bump(16)], dma=True)
                            ow = (QS.h, QS.v)
                        fin_state["xw"] = V1.v
                        fin_state["ost"] = QS.v

                for tw in range(4):
                    for c in range(64):
                        li, lslot, lw = lrot.acquire()
                        em("sync", lambda E, d=lslot, c=c, tw=tw: E.dma_start(
                            out=d, in_=z2v[c, :, tw * 1024:(tw + 1) * 1024]),
                           waits=[lw], incs=[SD[li].bump(16)], dma=True)
                        ld_tgt = SD[li].at()
                        for t in range(2):
                            pi, psx, pw = pool2.acquire()
                            em("tensor", lambda E, lslot=lslot, t=t, psx=psx: E.matmul(
                                psx, h64i2_sb[:], lslot[:, t * 512:(t + 1) * 512],
                                start=True, stop=True),
                               waits=[ld_tgt, pw], incs=[T1.bump(1)])
                            em("vector", lambda E, psx=psx, c=c, t=t: E.tensor_scalar_mul(
                                x2[:, c * 1024 + t * 512:c * 1024 + (t + 1) * 512],
                                psx, R64),
                               waits=[(T1.h, T1.v),
                                      (MM.h, down_done) if (tw > 0 and c == 0 and t == 0) else None],
                               incs=[EV2.bump(1)])
                            pool2.release(pi)
                        lrot.release(li)
                    x2_tgt = EV2.v
                    for o2c in range(16):
                        wi, wslab, ww = wdrot.acquire()
                        for q in range(4):
                            em("gpsimd", lambda E, wslab=wslab, o2c=o2c, q=q: E.dma_start(
                                out=wslab[:, q * 2048:(q + 1) * 2048].rearrange(
                                    "p (c o) -> p c o", c=16),
                                in_=wd_t[o2c, q * 16:(q + 1) * 16].transpose([1, 0, 2])),
                               waits=[ww if q == 0 else None],
                               incs=[SD[2 + wi].bump(16)], dma=True)
                        wq_tgt = SD[2 + wi].at()
                        sli, sslot, slw = slrot.acquire()
                        em("scalar", lambda E, d=sslot, o2c=o2c, tw=tw: E.dma_start(
                            out=d, in_=s_scr[o2c * 128:(o2c + 1) * 128,
                                             tw * 1024:(tw + 1) * 1024]),
                           waits=[slw], incs=[SD[4 + sli].bump(16)], dma=True)
                        sl_tgt = SD[4 + sli].at()
                        for th in range(2):
                            pi, psx, pw = dpool.acquire()

                            def dmm(E, wslab=wslab, th=th, psx=psx):
                                for c in range(64):
                                    ins = E.matmul(
                                        psx, wslab[:, c * 128:(c + 1) * 128],
                                        x2[:, c * 1024 + th * 512:c * 1024 + (th + 1) * 512],
                                        start=(c == 0), stop=(c == 63))
                                return ins
                            em("tensor", dmm,
                               waits=[wq_tgt, pw,
                                      (EV2.h, x2_tgt) if (o2c == 0 and th == 0) else None],
                               incs=[T1.bump(1)] + ([MM.bump(1)] if th == 1 else []))
                            d_tgt = T1.v
                            gi2, gslot, gw2 = grot.acquire()
                            em("scalar", lambda E, psx=psx, d=gslot, o2c=o2c:
                               E.activation(d, psx, AF.Silu,
                                            scale=wsdp_sb[:, o2c:o2c + 1]),
                               waits=[(T1.h, d_tgt), gw2], incs=[EV.bump(1)])
                            dpool.release(pi)
                            ri, rslot, rw2 = rrot.acquire()
                            em("vector", lambda E, d=rslot, g=gslot, ss=sslot, th=th:
                               E.tensor_mul(d, g, ss[:, th * 512:(th + 1) * 512]),
                               waits=[(EV.h, EV.v), sl_tgt, rw2],
                               incs=[V1.bump(1)])
                            grot.release(gi2)
                            if th == 1:
                                slrot.release(sli)
                            em("sync", lambda E, s=rslot, o2c=o2c, tw=tw, th=th:
                               E.dma_start(out=r_scr[o2c, tw * 2 + th], in_=s),
                               waits=[(V1.h, V1.v)], incs=[rrot.sems[ri].bump(16)], dma=True)
                            rrot.release(ri)
                        wdrot.release(wi)
                    down_done = MM.v
                    if tw > 0:
                        emit_final(2 * (tw - 1))
                        emit_final(2 * (tw - 1) + 1)
                emit_final(6)
                emit_final(7)
                em("gpsimd", lambda E: E.engine_nop(), waits=[(QS.h, QS.v)])
                em.gate(PH)
                em.flush(nc)

    return nc


def _prep(inputs):
    x = np.ascontiguousarray(np.asarray(inputs["L0_attn_hidden_states"], dtype=np.float32))
    w_qkv = np.asarray(inputs["L0_attn_w_qkv"], dtype=np.float32)
    w_o = np.asarray(inputs["L0_attn_w_o"], dtype=np.float32)
    su_ug = np.asarray(inputs["L0_ffn_SU_ug"], dtype=np.float32)
    su_dp = np.asarray(inputs["L0_ffn_SU_dp"], dtype=np.float32)
    ws_ug = np.asarray(inputs["L0_ffn_Wscale_ug"], dtype=np.float32)
    ws_dp = np.asarray(inputs["L0_ffn_Wscale_dp"], dtype=np.float32)
    W_up = np.asarray(inputs["L0_ffn_W_up"], dtype=np.float32)
    W_gate = np.asarray(inputs["L0_ffn_W_gate"], dtype=np.float32)
    W_down = np.asarray(inputs["L0_ffn_W_down"], dtype=np.float32)

    cm = np.ascontiguousarray
    d = {}
    d["wqkv_t"] = cm(w_qkv.T.reshape(16, 128, 24, 256).transpose(2, 0, 1, 3))
    d["wo_t"] = cm(w_o.T.reshape(16, 128, 4, 512).transpose(2, 0, 1, 3))
    d["wup_t"] = cm(W_up.T.reshape(128, 16, 8192).transpose(1, 0, 2)
                    .reshape(16, 128, 32, 256).transpose(2, 0, 1, 3))
    d["wgate_t"] = cm(W_gate.T.reshape(128, 16, 8192).transpose(1, 0, 2)
                      .reshape(16, 128, 32, 256).transpose(2, 0, 1, 3))
    d["wd_t"] = cm(W_down.T.reshape(128, 64, 2048).transpose(1, 0, 2)
                   .reshape(64, 128, 16, 128).transpose(2, 0, 1, 3))
    d["su_ug_t"] = cm(su_ug.reshape(16, 128).T)
    d["su_dp_t"] = cm(su_dp.reshape(64, 128).T)
    d["ws_up_t"] = cm(ws_ug[:INTER].reshape(64, 128).T)
    d["ws_gate_t"] = cm(ws_ug[INTER:].reshape(64, 128).T)
    d["ws_dp_t"] = cm(ws_dp.reshape(16, 128).T)
    d["h128"] = _hadamard(128)
    d["h16i8"] = cm(np.kron(_hadamard(16), np.eye(8, dtype=np.float32)))
    d["h64i2"] = cm(np.kron(_hadamard(64), np.eye(2, dtype=np.float32)))
    d["ident"] = np.eye(128, dtype=np.float32)
    d["ones"] = np.ones((128, 128), dtype=np.float32)
    cv = np.zeros((128, 8), dtype=np.float32)
    for i, v in enumerate((EPS_LN, -0.5, EPS_SM, SQRT_C)):
        cv[:, i] = v
    d["cvec"] = cv
    kp = np.arange(128)[:, None]
    qo = np.arange(512)[None, :]
    d["masks"] = np.stack(
        [np.where(dd * 128 + kp <= qo, 1.0, 0.0) for dd in range(4)]
    ).astype(np.float32)
    return x, d


def kernel(**inputs):
    x, consts = _prep(inputs)
    nc = build()
    in_maps = []
    for b in range(NCORES):
        m = dict(consts)
        m["x"] = x[b]
        in_maps.append(m)
    res = run_bass_kernel_spmd(nc, in_maps, core_ids=list(range(NCORES)))
    return np.stack([np.asarray(r["out"], dtype=np.float32) for r in res.results], axis=0)


# version 14
# speedup vs baseline: 1.0058x; 1.0058x over previous
import contextlib
import numpy as np
import concourse.bass as bass
import concourse.mybir as mybir
from concourse.bass_utils import run_bass_kernel_spmd

F32 = mybir.dt.float32
BF16 = mybir.dt.bfloat16
AF = mybir.ActivationFunctionType
ALU = mybir.AluOpType
AX = mybir.AxisListType

S = 4096
D = 2048
NH = 16
INTER = 8192
NCORES = 8

SQRT_C = float(128.0 ** -0.25)
R128 = float(128.0 ** -0.5)
R16 = 0.25
R64 = 0.125
EPS_LN = 1e-5
EPS_SM = 1e-35
NEG = -1.0e30

ENGINES = ("tensor", "vector", "scalar", "sync", "gpsimd")


def _hadamard(n):
    h = np.array([[1.0]], dtype=np.float32)
    while h.shape[0] < n:
        h = np.block([[h, h], [h, -h]])
    return np.ascontiguousarray(h, dtype=np.float32)


class Sem:
    def __init__(self, handle):
        self.h = handle
        self.v = 0

    def bump(self, k=1):
        self.v += k
        return (self.h, k, self.v)

    def at(self):
        return (self.h, self.v)


class Em:

    def __init__(self):
        self.ops = {e: [] for e in ENGINES}
        self._pend = {e: [] for e in ENGINES}
        self.n = 0
        self.paranoid = None

    def __call__(self, eng, fn, waits=(), incs=(), dma=False):
        w = self._pend[eng] + [x for x in waits if x is not None]
        if self.paranoid:
            w += [s.at() for s in self.paranoid]
        self._pend[eng] = []
        rec = {"fn": fn, "waits": w, "incs": list(incs), "dma": dma}
        self.ops[eng].append(rec)
        self.n += 1
        return rec

    def gate(self, ph):
        for e in ENGINES:
            if self.ops[e]:
                k = 16 if self.ops[e][-1]["dma"] else 1
                self.ops[e][-1]["incs"].append(ph.bump(k))
        tgt = ph.at()
        for e in ENGINES:
            self._pend[e] = [tgt]

    @staticmethod
    def _replay(recs, E):
        for rec in recs:
            best = {}
            for (sh, v) in rec["waits"]:
                if v > 0 and (sh.num not in best or best[sh.num][1] < v):
                    best[sh.num] = (sh, v)
            for sh, v in best.values():
                E.wait_ge(sh, v)
            ins = rec["fn"](E)
            incs = rec["incs"]
            if incs:
                sh, k, v = incs[0]
                ins.then_inc(sh, k)
                for (sh2, k2, _v2) in incs[1:]:
                    E.wait_ge(sh, v)
                    E.nop().then_inc(sh2, k2)

    def flush(self, nc):
        if not any(self.ops.values()):
            return
        ops = self.ops
        self.ops = {e: [] for e in ENGINES}
        with nc.Block() as blk:
            for e in ENGINES:
                if ops[e]:
                    getattr(blk, e)(lambda E, recs=ops[e]: self._replay(recs, E))


class Rot:

    def __init__(self, slots, sem):
        self.slots = slots
        self.sems = sem if isinstance(sem, list) else [sem] * len(slots)
        self.free_at = [0] * len(slots)
        self.i = 0

    def acquire(self):
        i = self.i % len(self.slots)
        self.i += 1
        v = self.free_at[i]
        return i, self.slots[i], ((self.sems[i].h, v) if v > 0 else None)

    def release(self, i):
        self.free_at[i] = self.sems[i].v


def build(upto=99, debug=()):
    nc = bass.Bass("TRN2", target_bir_lowering=False)

    dt_in = lambda n, s: nc.dram_tensor(n, s, F32, kind="ExternalInput").ap()
    x_in = dt_in("x", [S, D])
    wqkv_t = dt_in("wqkv_t", [24, 16, 128, 256])
    wo_t = dt_in("wo_t", [4, 16, 128, 512])
    wup_t = dt_in("wup_t", [32, 16, 128, 256])
    wgate_t = dt_in("wgate_t", [32, 16, 128, 256])
    wd_t = dt_in("wd_t", [16, 64, 128, 128])
    su_ug_in = dt_in("su_ug_t", [128, 16])
    su_dp_in = dt_in("su_dp_t", [128, 64])
    ws_up_in = dt_in("ws_up_t", [128, 64])
    ws_gate_in = dt_in("ws_gate_t", [128, 64])
    ws_dp_in = dt_in("ws_dp_t", [128, 16])
    h128_in = dt_in("h128", [128, 128])
    h16i8_in = dt_in("h16i8", [128, 128])
    h64i2_in = dt_in("h64i2", [128, 128])
    ident_in = dt_in("ident", [128, 128])
    ones_in = dt_in("ones", [128, 128])
    masks_in = dt_in("masks", [4, 128, 512])
    cvec_in = dt_in("cvec", [128, 8])

    out = nc.dram_tensor("out", [S, D], F32, kind="ExternalOutput").ap()

    def scratch(name, shape):
        kind = "ExternalOutput" if name in debug else "Internal"
        return nc.dram_tensor(name, shape, BF16, kind=kind).ap()

    z1_scr = scratch("z1_scr", [D, S])
    qkv_scr = scratch("qkv_scr", [3 * D, S])
    s_scr = scratch("s_scr", [D, S])
    z2_scr = scratch("z2_scr", [INTER, S])
    r_scr = scratch("r_scr", [16, 8, 128, 512])
    bc_scr = nc.dram_tensor("bc_scr", [2, 512], F32, kind="Internal").ap()

    em = Em()
    with contextlib.ExitStack() as st:
        sb = lambda n, sh, dt: st.enter_context(nc.sbuf_tensor(n, sh, dt))
        ps_ = lambda n, sh, dt: st.enter_context(nc.psum_tensor(n, sh, dt))
        sem = lambda n: Sem(st.enter_context(nc.semaphore(n)))

        h128su_sb = sb("h128su", [128, 2048], BF16)
        h128_sb = sb("h128pm", [128, 128], BF16)
        h16i8_sb = sb("h16i8_c", [128, 128], BF16)
        h64i2_sb = sb("h64i2_c", [128, 128], BF16)
        ident_sb = sb("identb", [128, 128], BF16)
        onesb_sb = sb("onesb", [128, 128], BF16)
        onesf_sb = sb("onesf", [128, 128], F32)
        bias_tab = sb("bias_tab", [128, 16], F32)
        nrm_sb = sb("nrm_sb", [128, 288], F32)
        su_ug_sb = sb("su_ug", [128, 16], F32)
        supeff_sb = sb("supeff", [128, 64], F32)
        wsgate_sb = sb("wsgate", [128, 64], F32)
        wsdp_sb = sb("wsdp", [128, 16], F32)
        sudp_sb = sb("sudp", [128, 64], F32)
        cvec_sb = sb("cvec_c", [128, 8], F32)
        RES = sb("RES", [128, 65536], BF16)
        _ACT_CONSTS = (EPS_LN, -0.5, EPS_SM, SQRT_C)
        for _i, _v in enumerate(_ACT_CONSTS):
            nc.const_aps.register(F32, float(_v), cvec_sb[:, _i:_i + 1])

        psA = ps_("psA", [128, 1024], F32)
        psB = ps_("psB", [128, 1024], F32)
        psC1 = ps_("psC1", [128, 512], F32)
        psC2 = ps_("psC2", [128, 512], F32)
        psD = ps_("psD", [128, 512], F32)
        psT = ps_("psT", [128, 1024], BF16)

        LD = sem("LD")
        WQ = sem("WQ")
        MM = sem("MM")
        T1 = sem("T1")
        EV = sem("EV")
        EV2 = sem("EV2")
        V1 = sem("V1")
        A1 = sem("A1")
        XD = sem("XD")
        XC = sem("XC")
        QS = sem("QS")
        PS = sem("PS")
        PB = sem("PB")
        PX = sem("PX")
        PR = sem("PR")
        PH = sem("PH")
        SD = [sem(f"SD{i}") for i in range(8)]

        ptrot = Rot([psT[:, 0:1024]], V1)

        def dmac(dst, src):
            return lambda E, d=dst, s=src: E.dma_start(out=d, in_=s)

        for dst, src in ((h128_sb[:], h128_in), (h16i8_sb[:], h16i8_in),
                         (h64i2_sb[:], h64i2_in), (ident_sb[:], ident_in),
                         (onesb_sb[:], ones_in)):
            em("gpsimd", dmac(dst, src), incs=[LD.bump(16)], dma=True)
        em("scalar", dmac(onesf_sb[:], ones_in), incs=[LD.bump(16)], dma=True)
        for dst, src in ((su_ug_sb[:], su_ug_in), (supeff_sb[:], ws_up_in),
                         (wsgate_sb[:], ws_gate_in), (wsdp_sb[:], ws_dp_in),
                         (sudp_sb[:], su_dp_in), (cvec_sb[:], cvec_in)):
            em("scalar", dmac(dst, src), incs=[LD.bump(16)], dma=True)
        em("vector", lambda E: E.tensor_mul(supeff_sb[:], supeff_sb[:], sudp_sb[:]),
           waits=[LD.at()], incs=[V1.bump(1)])
        for c in range(16):
            em("vector", lambda E, c=c: E.tensor_scalar_mul(
                h128su_sb[:, c * 128:(c + 1) * 128], h128_sb[:],
                su_ug_sb[:, c:c + 1]),
               incs=[V1.bump(1)] if c == 15 else ())
        em.gate(PH)
        em.flush(nc)

        xn_fm = RES
        if upto >= 1:
            with (nc.sbuf_tensor("xf", [128, 2 * D], F32) as xf,
                  nc.sbuf_tensor("xntm", [128, 2 * D], BF16) as xntm,
                  nc.sbuf_tensor("lnst", [128, 64], F32) as lnst,
                  nc.sbuf_tensor("wqs", [128, 2 * 4096], BF16) as wqs,
                  nc.sbuf_tensor("qst", [128, 2 * 4096], BF16) as qst,
                  nc.sbuf_tensor("sqb", [128, 1024], BF16) as sqb):
                tmrot = Rot([xntm[:, 0:D], xntm[:, D:2 * D]], T1)
                for t in range(32):
                    sl = t % 2
                    xfs = xf[:, sl * D:(sl + 1) * D]
                    ln = lnst[:, sl * 32:sl * 32 + 32]
                    em("sync", lambda E, d=xfs, t=t: E.dma_start(
                        out=d, in_=x_in[t * 128:(t + 1) * 128, :]),
                       waits=[(XC.h, t - 1) if t >= 2 else None],
                       incs=[SD[sl].bump(16)], dma=True)

                    def bn(E, xfs=xfs, ln=ln):
                        for i in range(4):
                            ins = E.bn_stats(ln[:, i * 6:(i + 1) * 6],
                                             xfs[:, i * 512:(i + 1) * 512])
                        return ins
                    em("vector", bn, waits=[SD[sl].at()], incs=[V1.bump(1)])
                    em("vector", lambda E, ln=ln: E.bn_aggr(ln[:, 24:26], ln[:, 0:24]),
                       waits=[V1.at()], incs=[V1.bump(1)])
                    em("scalar", lambda E, ln=ln: E.activation(
                        ln[:, 26:27], ln[:, 25:26], AF.Ln, bias=EPS_LN),
                       waits=[V1.at()], incs=[A1.bump(1)])
                    em("scalar", lambda E, ln=ln: E.activation(
                        ln[:, 27:28], ln[:, 26:27], AF.Exp, scale=-0.5),
                       waits=[A1.at()], incs=[A1.bump(1)])
                    ti, xts, tw = tmrot.acquire()
                    em("vector", lambda E, xfs=xfs, xts=xts, ln=ln: E.tensor_scalar(
                        xts, xfs, ln[:, 24:25], ln[:, 27:28],
                        ALU.subtract, ALU.mult),
                       waits=[A1.at(), tw], incs=[V1.bump(1), XC.bump(1)])
                    xn_ready = V1.v
                    for half in range(2):
                        pi, pslot, pw = ptrot.acquire()

                        def trs(E, xts=xts, half=half, pslot=pslot):
                            for b in range(8):
                                c = half * 8 + b
                                ins = E.transpose(pslot[:, b * 128:(b + 1) * 128],
                                                  xts[:, c * 128:(c + 1) * 128],
                                                  ident_sb[:])
                            return ins
                        em("tensor", trs, waits=[(V1.h, xn_ready), pw],
                           incs=[T1.bump(1)])

                        def tev(E, t=t, half=half, pslot=pslot):
                            for b in range(8):
                                c = half * 8 + b
                                ins = E.tensor_copy(
                                    xn_fm[:, c * 4096 + t * 128:c * 4096 + (t + 1) * 128],
                                    pslot[:, b * 128:(b + 1) * 128])
                            return ins
                        em("vector", tev, waits=[T1.at()], incs=[V1.bump(1)])
                        ptrot.release(pi)
                    tmrot.release(ti)
                xn_done = V1.v
                nrm_v = [0]
                nq_v = [0, 0]
                pend_nm = [None]
                wrot = Rot([wqs[:, 0:4096], wqs[:, 4096:8192]], MM)
                srot = Rot([qst[:, 0:4096], qst[:, 4096:8192]], [SD[2], SD[3]])
                pool4 = Rot([psA[:, 0:512], psA[:, 512:1024],
                             psB[:, 0:512], psB[:, 512:1024]], EV)
                for ob in range(24):
                    wi, wslot, ww = wrot.acquire()
                    em("gpsimd", lambda E, d=wslot, ob=ob: E.dma_start(
                        out=d.rearrange("p (fc o) -> p fc o", fc=16),
                        in_=wqkv_t[ob].transpose([1, 0, 2])),
                       waits=[ww], incs=[SD[wi].bump(16)], dma=True)
                    wq_tgt = SD[wi].at()
                    for oi in range(2):
                        oc = ob * 2 + oi
                        scale = SQRT_C if oc < 32 else 1.0
                        si, estage, sw = srot.acquire()
                        for t in range(8):
                            pi, psx, pw = pool4.acquire()

                            def mms(E, wslot=wslot, oi=oi, t=t, psx=psx):
                                for fc in range(16):
                                    ins = E.matmul(
                                        psx,
                                        wslot[:, fc * 256 + oi * 128:fc * 256 + (oi + 1) * 128],
                                        xn_fm[:, fc * 4096 + t * 512:fc * 4096 + (t + 1) * 512],
                                        start=(fc == 0), stop=(fc == 15))
                                return ins
                            em("tensor", mms, waits=[wq_tgt, pw,
                               (V1.h, xn_done) if (ob == 0 and oi == 0 and t == 0) else None],
                               incs=[T1.bump(1)] + ([MM.bump(1)] if (oi == 1 and t == 7) else []))
                            em("scalar", lambda E, psx=psx, d=estage, t=t, scale=scale:
                               E.activation(d[:, t * 512:(t + 1) * 512], psx,
                                            AF.Copy, scale=scale),
                               waits=[(T1.h, T1.v), sw if t == 0 else None],
                               incs=([] if oc < 32 else [EV.bump(1)]))
                            if pend_nm[0] is not None:
                                qsl2, oc2, t2, ev2 = pend_nm[0]
                                pend_nm[0] = None
                                em("tensor", lambda E, qsl=qsl2: E.matmul(
                                    psC2[0:1, :], onesb_sb[:, 0:1],
                                    sqb[:, qsl * 512:(qsl + 1) * 512],
                                    start=True, stop=True),
                                   waits=[(EV.h, ev2), (V1.h, nrm_v[0])],
                                   incs=[T1.bump(1)])
                                em("vector", lambda E, oc=oc2, t=t2: E.reduce_max(
                                    nrm_sb[0:1, oc * 8 + t:oc * 8 + t + 1],
                                    psC2[0:1, :], axis=AX.X),
                                   waits=[(T1.h, T1.v)], incs=[V1.bump(1)])
                                nrm_v[0] = V1.v
                                nq_v[qsl2] = V1.v
                            if oc < 32:
                                qsl = t % 2
                                em("scalar", lambda E, psx=psx, qsl=qsl:
                                   E.activation(sqb[:, qsl * 512:(qsl + 1) * 512],
                                                psx, AF.Square),
                                   waits=[(V1.h, nq_v[qsl])], incs=[EV.bump(1)])
                                pend_nm[0] = (qsl, oc, t, EV.v)
                            pool4.release(pi)
                        em("sync", lambda E, s=estage, oc=oc: E.dma_start(
                            out=qkv_scr[oc * 128:(oc + 1) * 128, :], in_=s),
                           waits=[(EV.h, EV.v)], incs=[srot.sems[si].bump(16)], dma=True)
                        srot.release(si)
                    wrot.release(wi)
                if pend_nm[0] is not None:
                    qsl2, oc2, t2, ev2 = pend_nm[0]
                    pend_nm[0] = None
                    em("tensor", lambda E, qsl=qsl2: E.matmul(
                        psC2[0:1, :], onesb_sb[:, 0:1],
                        sqb[:, qsl * 512:(qsl + 1) * 512],
                        start=True, stop=True),
                       waits=[(EV.h, ev2), (V1.h, nrm_v[0])], incs=[T1.bump(1)])
                    em("vector", lambda E, oc=oc2, t=t2: E.reduce_max(
                        nrm_sb[0:1, oc * 8 + t:oc * 8 + t + 1],
                        psC2[0:1, :], axis=AX.X),
                       waits=[(T1.h, T1.v)], incs=[V1.bump(1)])
                    nrm_v[0] = V1.v
                norm_done = nrm_v[0]
                for h in range(NH):
                    em("vector", lambda E, h=h: E.reduce_max(
                        nrm_sb[0:1, 256 + (h % 2):256 + (h % 2) + 1],
                        nrm_sb[0:1, h * 8:(h + 1) * 8], axis=AX.X),
                       waits=[(V1.h, norm_done) if h == 0 else V1.at()],
                       incs=[V1.bump(1)])
                    em("vector", lambda E, h=h: E.reduce_max(
                        nrm_sb[0:1, 258:259],
                        nrm_sb[0:1, (16 + h) * 8:(17 + h) * 8], axis=AX.X),
                       waits=[V1.at()], incs=[V1.bump(1)])
                    em("vector", lambda E, h=h: E.tensor_add(
                        nrm_sb[0:1, 259:260], nrm_sb[0:1, 256 + (h % 2):256 + (h % 2) + 1],
                        nrm_sb[0:1, 258:259]),
                       waits=[V1.at()], incs=[V1.bump(1)])
                    em("vector", lambda E, h=h: E.tensor_scalar_mul(
                        nrm_sb[0:1, 260 + h:261 + h],
                        nrm_sb[0:1, 259:260], -0.5025 * SQRT_C * SQRT_C),
                       waits=[V1.at()], incs=[V1.bump(1)])
                    em("tensor", lambda E, h=h: E.matmul(
                        psA[:, h:h + 1] if h % 2 == 0 else psB[:, h:h + 1],
                        onesf_sb[0:1, :],
                        nrm_sb[0:1, 260 + h:261 + h],
                        start=True, stop=True),
                       waits=[V1.at(), (A1.h, A1.v),
                              (EV.h, EV.v) if h < 2 else None],
                       incs=[T1.bump(1)])
                    em("scalar", lambda E, h=h: E.activation(
                        bias_tab[:, h:h + 1],
                        (psA if h % 2 == 0 else psB)[:, h:h + 1], AF.Copy),
                       waits=[(T1.h, T1.v)], incs=[A1.bump(1)])
                bias_done = A1.v
                if "bias_dbg" in debug:
                    bias_dbg = nc.dram_tensor("bias_dbg", [128, 288], F32,
                                              kind="ExternalOutput").ap()
                    em("scalar", lambda E: E.dma_start(
                        out=bias_dbg[:, 0:16], in_=bias_tab[:]),
                       waits=[(A1.h, bias_done)], incs=[PS.bump(16)], dma=True)
                    em("scalar", lambda E: E.dma_start(
                        out=bias_dbg[:, 16:304] if False else bias_dbg[:, 0:288].bitcast(F32),
                        in_=nrm_sb[:]) if False else E.dma_start(
                        out=bias_dbg[:], in_=nrm_sb[:]),
                       waits=[(V1.h, V1.v)], incs=[PS.bump(16)], dma=True)
                pool2 = Rot([psC1[:], psC2[:]], EV2)
                for c in range(16):
                    si, zstage, sw = srot.acquire()
                    for t in range(8):
                        pi, psx, pw = pool2.acquire()
                        em("tensor", lambda E, c=c, t=t, psx=psx: E.matmul(
                            psx, h128su_sb[:, c * 128:(c + 1) * 128],
                            xn_fm[:, c * 4096 + t * 512:c * 4096 + (t + 1) * 512],
                            start=True, stop=True),
                           waits=[pw, (V1.h, norm_done) if (c == 0 and t <= 1) else None],
                           incs=[T1.bump(1)])
                        em("vector", lambda E, psx=psx, d=zstage, t=t:
                           E.tensor_scalar_mul(d[:, t * 512:(t + 1) * 512], psx, R128),
                           waits=[(T1.h, T1.v), sw if t == 0 else None],
                           incs=[EV2.bump(1)])
                        pool2.release(pi)
                    em("sync", lambda E, s=zstage, c=c: E.dma_start(
                        out=z1_scr[c * 128:(c + 1) * 128, :], in_=s),
                       waits=[(EV2.h, EV2.v)], incs=[srot.sems[si].bump(16)], dma=True)
                    srot.release(si)
                em.gate(PH)
                em.flush(nc)

        attnT = RES
        import os
        _TC = bool(os.environ.get("TEST_C"))
        _TD = bool(os.environ.get("TEST_D"))
        if os.environ.get("PARANOID_P2"):
            em.paranoid = [MM, T1, V1, A1, PS, PB, PX, PR]
        if upto >= 3:
            with (nc.sbuf_tensor("qts", [128, 2 * 4096], BF16) as qts,
                  nc.sbuf_tensor("kts", [128, 2 * 4096], BF16) as kts,
                  nc.sbuf_tensor("vfm", [128, 2 * 1024], BF16) as vfm,
                  nc.sbuf_tensor("vtm", [128, 2 * 4096], BF16) as vtm,
                  nc.sbuf_tensor("probs", [128, 3072], BF16) as probs,
                  nc.sbuf_tensor("masks_c", [128, 2048], BF16) as mask_sb,
                  nc.sbuf_tensor("attst", [128, 32], F32) as attst,
                  nc.sbuf_tensor("rssb", [128, 2048], F32) as rssb):
                em("gpsimd", lambda E: E.dma_start(
                    out=mask_sb.rearrange("p (d f) -> p d f", d=4),
                    in_=masks_in.rearrange("d p f -> p d f")),
                   incs=[LD.bump(16)], dma=True)
                mask_ld = LD.at()
                hrot = Rot([0, 1], MM)
                recip_v = [0]
                bc_free = [None, None]
                pr_bank = [0, 0]
                vrot = Rot([vfm[:, 0:1024], vfm[:, 1024:2048]], T1)
                prot = Rot([probs[:, 0:1024], probs[:, 1024:2048],
                            probs[:, 2048:3072]], PX)
                wvrot = Rot([psA, psB], PB)
                for h in range(NH):
                    hi, hs, hw = hrot.acquire()
                    qQ = qts[:, hs * 4096:(hs + 1) * 4096]
                    kK = kts[:, hs * 4096:(hs + 1) * 4096]
                    vT = vtm[:, hs * 4096:(hs + 1) * 4096]
                    em("sync", lambda E, d=qQ, h=h: E.dma_start(
                        out=d, in_=qkv_scr[h * 128:(h + 1) * 128, :]),
                       waits=[hw], incs=[SD[hs].bump(16)], dma=True)
                    em("sync", lambda E, d=kK, h=h: E.dma_start(
                        out=d, in_=qkv_scr[(16 + h) * 128:(17 + h) * 128, :]),
                       incs=[SD[2 + hs].bump(16)], dma=True)
                    qk_tgt = [SD[hs].at(), SD[2 + hs].at()]
                    for quar in range(4):
                        vi, vslot, vw = vrot.acquire()
                        em("sync", lambda E, d=vslot, h=h, quar=quar: E.dma_start(
                            out=d, in_=qkv_scr[(32 + h) * 128:(33 + h) * 128,
                                               quar * 1024:(quar + 1) * 1024]),
                           waits=[vw], incs=[SD[4 + vi].bump(16)], dma=True)
                        v_tgt = SD[4 + vi].at()
                        pi, pslot, pw = ptrot.acquire()

                        def vtr(E, vslot=vslot, pslot=pslot):
                            for b in range(8):
                                ins = E.transpose(pslot[:, b * 128:(b + 1) * 128],
                                                  vslot[:, b * 128:(b + 1) * 128],
                                                  ident_sb[:])
                            return ins
                        em("tensor", vtr, waits=[v_tgt, pw],
                           incs=[T1.bump(1)])
                        vrot.release(vi)

                        def vev(E, vT=vT, quar=quar, pslot=pslot):
                            for b in range(8):
                                kc = quar * 8 + b
                                ins = E.tensor_copy(vT[:, kc * 128:(kc + 1) * 128],
                                                    pslot[:, b * 128:(b + 1) * 128])
                            return ins
                        em("vector", vev, waits=[T1.at()], incs=[V1.bump(1)])
                        ptrot.release(pi)

                    if "vtm_dbg" in debug and h == 1:
                        vtm_dbg = nc.dram_tensor("vtm_dbg", [128, 4096], BF16,
                                                 kind="ExternalOutput").ap()
                        em("sync", lambda E, vT=vT: E.dma_start(out=vtm_dbg, in_=vT),
                           waits=[(V1.h, V1.v)], incs=[PS.bump(16)], dma=True)
                    bias_col = bias_tab[:, h:h + 1]
                    bias_tgt = 0

                    for qg in range(8):
                        nchunk = 4 * qg + 4
                        nwave = nchunk // 2
                        acc = psC1 if qg % 2 == 0 else psD
                        par = qg % 2
                        pend_av = []
                        for w_i in range(nwave):
                            k0 = 2 * w_i
                            ri, reg, rw = wvrot.acquire()
                            for j in range(2):
                                kc = k0 + j
                                em("tensor", lambda E, kc=kc, j=j, reg=reg, kK=kK, qQ=qQ, qg=qg:
                                   E.matmul(reg[:, j * 512:(j + 1) * 512],
                                            kK[:, kc * 128:(kc + 1) * 128],
                                            qQ[:, qg * 512:(qg + 1) * 512],
                                            start=True, stop=True),
                                   waits=[rw if j == 0 else None,
                                          qk_tgt[0] if (qg == 0 and w_i == 0 and j == 0) else None,
                                          qk_tgt[1] if (qg == 0 and w_i == 0 and j == 1) else None],
                                   incs=[T1.bump(1)])
                            mm_tgt = T1.v
                            qi, wslot, qw = prot.acquire()
                            em("scalar", lambda E, reg=reg, wslot=wslot, bias_col=bias_col:
                               E.activation(wslot, reg[:], AF.Exp, bias=bias_col),
                               waits=[(T1.h, mm_tgt), qw],
                               incs=[PB.bump(1)])
                            wvrot.release(ri)
                            mask_tgt = None
                            for j in range(2):
                                dd = (k0 + j) - 4 * qg
                                if dd >= 0:
                                    em("vector", lambda E, j=j, wslot=wslot, dd=dd:
                                       E.tensor_mul(wslot[:, j * 512:(j + 1) * 512],
                                                    wslot[:, j * 512:(j + 1) * 512],
                                                    mask_sb[:, dd * 512:(dd + 1) * 512]),
                                       waits=[PB.at(), mask_ld], incs=[V1.bump(1)])
                                    mask_tgt = (V1.h, V1.v)

                            def avmm(E, k0=k0, wslot=wslot, vT=vT, nchunk=nchunk, acc=acc):
                                for j in range(2):
                                    kc = k0 + j
                                    E.matmul(acc[:], vT[:, kc * 128:(kc + 1) * 128],
                                             wslot[:, j * 512:(j + 1) * 512],
                                             start=(kc == 0), stop=(kc == nchunk - 1))
                                for j in range(2):
                                    kc = k0 + j
                                    ins = E.matmul(psC2[0:1, :], onesb_sb[:, 0:1],
                                                   wslot[:, j * 512:(j + 1) * 512],
                                                   start=(kc == 0), stop=(kc == nchunk - 1))
                                return ins
                            this_av = dict(
                                fn=avmm,
                                waits=[PB.at(), mask_tgt,
                                       (PR.h, pr_bank[par]) if w_i == 0 else None,
                                       (V1.h, recip_v[0]) if w_i == 0 else None])
                            pend_av.append(this_av)
                            if len(pend_av) > 2:
                                p2_ = pend_av.pop(0)
                                em("tensor", p2_["fn"], waits=p2_["waits"],
                                   incs=[PX.bump(1)])
                            prot.release(qi)
                            prot.free_at[qi] = PX.v + len(pend_av)
                        for p2_ in pend_av:
                            em("tensor", p2_["fn"], waits=p2_["waits"],
                               incs=[PX.bump(1)])
                        av_tgt = PX.v
                        em("vector", lambda E, par=par: E.reciprocal(
                            rssb[0:1, 1024 + par * 512:1536 + par * 512],
                            psC2[0:1, :]),
                           waits=[(PX.h, av_tgt)], incs=[V1.bump(1)])
                        recip_v[0] = V1.v
                        em("gpsimd", lambda E, par=par: E.dma_start(
                            out=bc_scr[par:par + 1, :],
                            in_=rssb[0:1, 1024 + par * 512:1536 + par * 512]),
                           waits=[(V1.h, recip_v[0]), bc_free[par]],
                           incs=[SD[6].bump(16)], dma=True)
                        d1 = SD[6].at()
                        em("gpsimd", lambda E, par=par: E.dma_start(
                            out=rssb[:, par * 512:(par + 1) * 512],
                            in_=bc_scr[par:par + 1, :].broadcast_to([128, 512])),
                           waits=[d1], incs=[SD[7].bump(16)], dma=True)
                        d2 = SD[7].at()
                        em("vector", lambda E, h=h, qg=qg, par=par, acc=acc:
                           E.tensor_mul(
                               attnT[:, h * 4096 + qg * 512:h * 4096 + (qg + 1) * 512],
                               acc[:], rssb[:, par * 512:(par + 1) * 512]),
                           waits=[d2], incs=[V1.bump(1), PR.bump(1)])
                        bc_free[par] = (V1.h, V1.v)
                        pr_bank[par] = PR.v
                    em("tensor", lambda E: E.matmul(
                        psC2[0:1, 0:1], onesb_sb[0:1, 0:1], onesb_sb[0:1, 0:1],
                        start=True, stop=True),
                       waits=[(PX.h, PX.v), (V1.h, recip_v[0])],
                       incs=[T1.bump(1), MM.bump(1)])
                    hrot.release(hi)
                em.paranoid = None
                if "attnT_dbg" in debug:
                    att_dbg = nc.dram_tensor("attnT_dbg", [D, S], BF16,
                                             kind="ExternalOutput").ap()
                    for cc in range(16):
                        em("sync", lambda E, cc=cc: E.dma_start(
                            out=att_dbg[cc * 128:(cc + 1) * 128, :],
                            in_=attnT[:, cc * 4096:(cc + 1) * 4096]),
                           waits=[(PR.h, PR.v) if cc == 0 else None],
                           incs=[QS.bump(16)], dma=True)
                em.gate(PH)
                em.flush(nc)

        if upto >= 4:
            with (nc.sbuf_tensor("wos", [128, 2 * 8192], BF16) as wos,
                  nc.sbuf_tensor("sst", [128, 2 * 4096], BF16) as sst):
                wrot = Rot([wos[:, 0:8192], wos[:, 8192:16384]], MM)
                srot = Rot([sst[:, 0:4096], sst[:, 4096:8192]], [SD[2], SD[3]])
                pool4 = Rot([psA[:, 0:512], psA[:, 512:1024],
                             psB[:, 0:512], psB[:, 512:1024]], EV)
                for ob in range(4):
                    wi, wslot, ww = wrot.acquire()
                    em("gpsimd", lambda E, d=wslot, ob=ob: E.dma_start(
                        out=d.rearrange("p (fc o) -> p fc o", fc=16),
                        in_=wo_t[ob].transpose([1, 0, 2])),
                       waits=[ww], incs=[SD[wi].bump(16)], dma=True)
                    wq_tgt = SD[wi].at()
                    for oi in range(4):
                        oc = ob * 4 + oi
                        si, estage, sw = srot.acquire()
                        for t in range(8):
                            pi, psx, pw = pool4.acquire()

                            def mms(E, wslot=wslot, oi=oi, t=t, psx=psx):
                                for fc in range(16):
                                    ins = E.matmul(
                                        psx,
                                        wslot[:, fc * 512 + oi * 128:fc * 512 + (oi + 1) * 128],
                                        attnT[:, fc * 4096 + t * 512:fc * 4096 + (t + 1) * 512],
                                        start=(fc == 0), stop=(fc == 15))
                                return ins
                            em("tensor", mms, waits=[wq_tgt, pw],
                               incs=[T1.bump(1)] + ([MM.bump(1)] if (oi == 3 and t == 7) else []))
                            em("scalar", lambda E, psx=psx, d=estage, t=t:
                               E.activation(d[:, t * 512:(t + 1) * 512], psx, AF.Copy),
                               waits=[(T1.h, T1.v), sw if t == 0 else None],
                               incs=[EV.bump(1)])
                            pool4.release(pi)
                        em("sync", lambda E, s=estage, oc=oc: E.dma_start(
                            out=s_scr[oc * 128:(oc + 1) * 128, :], in_=s),
                           waits=[(EV.h, EV.v)], incs=[srot.sems[si].bump(16)], dma=True)
                        srot.release(si)
                    wrot.release(wi)
                em.gate(PH)
                em.flush(nc)

        x1 = RES
        if upto >= 5:
            with nc.sbuf_tensor("zld", [128, 2 * 4096], BF16) as zld:
                lrot = Rot([zld[:, 0:4096], zld[:, 4096:8192]], T1)
                pool2 = Rot([psC1[:], psC2[:]], EV2)
                z1v = z1_scr.rearrange("(p c) t -> c p t", c=16)
                for c in range(16):
                    li, lslot, lw = lrot.acquire()
                    em("sync", lambda E, d=lslot, c=c: E.dma_start(out=d, in_=z1v[c]),
                       waits=[lw], incs=[SD[li].bump(16)], dma=True)
                    ld_tgt = SD[li].at()
                    for t in range(8):
                        pi, psx, pw = pool2.acquire()
                        em("tensor", lambda E, lslot=lslot, t=t, psx=psx: E.matmul(
                            psx, h16i8_sb[:], lslot[:, t * 512:(t + 1) * 512],
                            start=True, stop=True),
                           waits=[ld_tgt, pw], incs=[T1.bump(1)])
                        em("vector", lambda E, psx=psx, c=c, t=t: E.tensor_scalar_mul(
                            x1[:, c * 4096 + t * 512:c * 4096 + (t + 1) * 512], psx, R16),
                           waits=[(T1.h, T1.v)], incs=[EV2.bump(1)])
                        pool2.release(pi)
                    lrot.release(li)
                em.gate(PH)
                em.flush(nc)

        if upto >= 6:
            with (nc.sbuf_tensor("wub", [128, 2 * 4096], BF16) as wub,
                  nc.sbuf_tensor("wgb", [128, 2 * 4096], BF16) as wgb,
                  nc.sbuf_tensor("ug32", [128, 2048], F32) as ug32,
                  nc.sbuf_tensor("ustg", [128, 2 * 512], BF16) as ustg,
                  nc.sbuf_tensor("zst", [128, 2 * 4096], BF16) as zst):
                wurot = Rot([wub[:, 0:4096], wub[:, 4096:8192]], MM)
                wgrot = Rot([wgb[:, 0:4096], wgb[:, 4096:8192]], MM)
                zrot = Rot([zst[:, 0:4096], zst[:, 4096:8192]], [SD[4], SD[5]])
                uprot = Rot([psA[:, 0:512], psA[:, 512:1024]], EV)
                gprot = Rot([psB[:, 0:512], psB[:, 512:1024]], EV)
                zprot = Rot([psC1[:], psC2[:]], EV2)
                ugrot = Rot([ug32[:, 0:1024], ug32[:, 1024:2048]], V1)
                usrot = Rot([ustg[:, 0:512], ustg[:, 512:1024]], T1)
                u_dbg = None
                if "u_dbg" in debug:
                    u_dbg = nc.dram_tensor("u_dbg", [INTER, S], BF16,
                                           kind="ExternalOutput").ap()
                pend_z = None
                for jb in range(32):
                    ui, wu, uw = wurot.acquire()
                    gi, wg, gw = wgrot.acquire()
                    em("gpsimd", lambda E, d=wu, jb=jb: E.dma_start(
                        out=d.rearrange("p (c j) -> p c j", c=16),
                        in_=wup_t[jb].transpose([1, 0, 2])),
                       waits=[uw], incs=[SD[ui].bump(16)], dma=True)
                    em("gpsimd", lambda E, d=wg, jb=jb: E.dma_start(
                        out=d.rearrange("p (c j) -> p c j", c=16),
                        in_=wgate_t[jb].transpose([1, 0, 2])),
                       waits=[gw], incs=[SD[2 + gi].bump(16)], dma=True)
                    wq_tgt = [SD[ui].at(), SD[2 + gi].at()]
                    for ji in range(2):
                        jc = jb * 2 + ji
                        zi, zstage, zw = zrot.acquire()
                        for t in range(8):
                            pu_i, psu, puw = uprot.acquire()
                            pg_i, psg, pgw = gprot.acquire()

                            def ummd(E, wu=wu, ji=ji, t=t, psu=psu):
                                for c in range(16):
                                    ins = E.matmul(
                                        psu,
                                        wu[:, c * 256 + ji * 128:c * 256 + (ji + 1) * 128],
                                        x1[:, c * 4096 + t * 512:c * 4096 + (t + 1) * 512],
                                        start=(c == 0), stop=(c == 15))
                                return ins
                            em("tensor", ummd, waits=wq_tgt + [puw],
                               incs=[T1.bump(1)])
                            up_tgt = T1.v

                            def gmmd(E, wg=wg, ji=ji, t=t, psg=psg):
                                for c in range(16):
                                    ins = E.matmul(
                                        psg,
                                        wg[:, c * 256 + ji * 128:c * 256 + (ji + 1) * 128],
                                        x1[:, c * 4096 + t * 512:c * 4096 + (t + 1) * 512],
                                        start=(c == 0), stop=(c == 15))
                                return ins
                            em("tensor", gmmd, waits=[pgw], incs=[T1.bump(1)])
                            gt_tgt = T1.v

                            si2, ugslot, sw2 = ugrot.acquire()
                            upf = ugslot[:, 0:512]
                            gaf = ugslot[:, 512:1024]
                            em("scalar", lambda E, psu=psu, d=upf, jc=jc:
                               E.activation(d, psu, AF.Copy,
                                            scale=supeff_sb[:, jc:jc + 1]),
                               waits=[(T1.h, up_tgt), sw2], incs=[EV.bump(1)])
                            uprot.release(pu_i)
                            em("scalar", lambda E, psg=psg, d=gaf, jc=jc:
                               E.activation(d, psg, AF.Silu,
                                            scale=wsgate_sb[:, jc:jc + 1]),
                               waits=[(T1.h, gt_tgt)], incs=[EV.bump(1)])
                            gprot.release(pg_i)
                            usi, ut, usw = usrot.acquire()
                            em("vector", lambda E, d=ut, a=upf, b=gaf:
                               E.tensor_mul(d, a, b),
                               waits=[(EV.h, EV.v), usw], incs=[V1.bump(1)])
                            ugrot.release(si2)
                            u_tgt = V1.v
                            if u_dbg is not None:
                                em("scalar", lambda E, s=ut, jc=jc, t=t: E.dma_start(
                                    out=u_dbg[jc * 128:(jc + 1) * 128,
                                              t * 512:(t + 1) * 512], in_=s),
                                   waits=[(V1.h, u_tgt)], incs=[PS.bump(16)], dma=True)
                            pz_i, psz, pzw = zprot.acquire()

                            def do_z(E, ps=psz, u=ut):
                                return E.matmul(ps, h128_sb[:], u,
                                                start=True, stop=True)
                            this_z = dict(fn=do_z, waits=[(V1.h, u_tgt), pzw],
                                          t=t, psz=psz, usi=usi, pz_i=pz_i,
                                          zstage=zstage, zw=zw, jc=jc, zi=zi)
                            if pend_z is not None:
                                p = pend_z
                                em("tensor", p["fn"], waits=p["waits"],
                                   incs=[T1.bump(1)] + ([MM.bump(1)] if p["t"] == 7 else []))
                                usrot.release(p["usi"])
                                em("vector", lambda E, ps=p["psz"], d=p["zstage"], t=p["t"]:
                                   E.tensor_scalar_mul(d[:, t * 512:(t + 1) * 512], ps, R128),
                                   waits=[(T1.h, T1.v),
                                          p["zw"] if p["t"] == 0 else None],
                                   incs=[EV2.bump(1)])
                                zprot.release(p["pz_i"])
                                if p["t"] == 7:
                                    em("sync", lambda E, s=p["zstage"], jc=p["jc"]:
                                       E.dma_start(
                                           out=z2_scr[jc * 128:(jc + 1) * 128, :],
                                           in_=s),
                                       waits=[(EV2.h, EV2.v)],
                                       incs=[zrot.sems[p["zi"]].bump(16)], dma=True)
                                    zrot.release(p["zi"])
                            pend_z = this_z
                    wurot.release(ui)
                    wurot.free_at[ui] = MM.v + 1
                    wgrot.release(gi)
                    wgrot.free_at[gi] = MM.v + 1
                p = pend_z
                em("tensor", p["fn"], waits=p["waits"],
                   incs=[T1.bump(1), MM.bump(1)])
                usrot.release(p["usi"])
                em("vector", lambda E, ps=p["psz"], d=p["zstage"], t=p["t"]:
                   E.tensor_scalar_mul(d[:, t * 512:(t + 1) * 512], ps, R128),
                   waits=[(T1.h, T1.v)], incs=[EV2.bump(1)])
                zprot.release(p["pz_i"])
                em("sync", lambda E, s=p["zstage"], jc=p["jc"]: E.dma_start(
                    out=z2_scr[jc * 128:(jc + 1) * 128, :], in_=s),
                   waits=[(EV2.h, EV2.v)], incs=[zrot.sems[p["zi"]].bump(16)], dma=True)
                zrot.release(p["zi"])
                em.gate(PH)
                em.flush(nc)

        x2 = RES
        if upto >= 7:
            with (nc.sbuf_tensor("z2ld", [128, 2 * 1024], BF16) as z2ld,
                  nc.sbuf_tensor("wdb", [128, 2 * 8192], BF16) as wdb,
                  nc.sbuf_tensor("gsl", [128, 512], F32) as gsl,
                  nc.sbuf_tensor("ssl", [128, 2 * 1024], BF16) as ssl,
                  nc.sbuf_tensor("rsl", [128, 2 * 512], BF16) as rsl,
                  nc.sbuf_tensor("xw", [128, 2048], F32) as xw,
                  nc.sbuf_tensor("rbuf", [128, 8192], BF16) as rbuf,
                  nc.sbuf_tensor("ost", [128, 1024], F32) as ost):
                z2v = z2_scr.rearrange("(p c) t -> c p t", c=64)
                lrot = Rot([z2ld[:, 0:1024], z2ld[:, 1024:2048]], T1)
                pool2 = Rot([psC1[:], psC2[:]], EV2)
                wdrot = Rot([wdb[:, 0:8192], wdb[:, 8192:16384]], MM)
                dpool = Rot([psA[:, 0:512], psA[:, 512:1024],
                             psB[:, 0:512], psB[:, 512:1024]], EV)
                grot = Rot([gsl[:, 0:512]], V1)
                slrot = Rot([ssl[:, 0:1024], ssl[:, 1024:2048]], V1)
                rrot = Rot([rsl[:, 0:512], rsl[:, 512:1024]], [SD[6], SD[7]])
                down_done = MM.v
                rs_val = {}
                fin_state = {"rb": 0, "xw": 0, "ost": 0}

                def emit_final(tb5):
                    for o2c in range(16):
                        em("gpsimd", lambda E, o2c=o2c, tb5=tb5: E.dma_start(
                            out=rbuf[:, o2c * 512:(o2c + 1) * 512],
                            in_=r_scr[o2c, tb5]),
                           waits=[rs_val[(o2c, tb5)],
                                  (T1.h, fin_state["rb"]) if o2c == 0 else None],
                           incs=[XD.bump(16)], dma=True)
                    r_tgt = XD.at()
                    for sub in range(4):
                        tb = tb5 * 4 + sub
                        em("sync", lambda E, tb=tb: E.dma_start(
                            out=xw[:], in_=x_in[tb * 128:(tb + 1) * 128, :]),
                           waits=[(V1.h, fin_state["xw"])],
                           incs=[PS.bump(16)], dma=True)
                        x_tgt = PS.at()
                        ow = (QS.h, fin_state["ost"])
                        for g in range(2):
                            pi, pslot, pw = ptrot.acquire()

                            def trp(E, sub=sub, g=g, pslot=pslot):
                                for b in range(8):
                                    o2c = g * 8 + b
                                    ins = E.transpose(
                                        pslot[:, b * 128:(b + 1) * 128],
                                        rbuf[:, o2c * 512 + sub * 128:o2c * 512 + (sub + 1) * 128],
                                        ident_sb[:])
                                return ins
                            em("tensor", trp, waits=[r_tgt, pw], incs=[T1.bump(1)])
                            fin_state["rb"] = T1.v

                            def oadd(E, g=g, pslot=pslot):
                                for b in range(8):
                                    o2c = g * 8 + b
                                    ins = E.tensor_add(
                                        ost[:, b * 128:(b + 1) * 128],
                                        pslot[:, b * 128:(b + 1) * 128],
                                        xw[:, o2c * 128:(o2c + 1) * 128])
                                return ins
                            em("vector", oadd,
                               waits=[T1.at(), x_tgt if g == 0 else None, ow],
                               incs=[V1.bump(1)])
                            ptrot.release(pi)
                            em("sync", lambda E, tb=tb, g=g: E.dma_start(
                                out=out[tb * 128:(tb + 1) * 128,
                                        g * 1024:(g + 1) * 1024], in_=ost[:]),
                               waits=[(V1.h, V1.v)], incs=[QS.bump(16)], dma=True)
                            ow = (QS.h, QS.v)
                        fin_state["xw"] = V1.v
                        fin_state["ost"] = QS.v

                for tw in range(4):
                    for c in range(64):
                        li, lslot, lw = lrot.acquire()
                        em("sync", lambda E, d=lslot, c=c, tw=tw: E.dma_start(
                            out=d, in_=z2v[c, :, tw * 1024:(tw + 1) * 1024]),
                           waits=[lw], incs=[SD[li].bump(16)], dma=True)
                        ld_tgt = SD[li].at()
                        for t in range(2):
                            pi, psx, pw = pool2.acquire()
                            em("tensor", lambda E, lslot=lslot, t=t, psx=psx: E.matmul(
                                psx, h64i2_sb[:], lslot[:, t * 512:(t + 1) * 512],
                                start=True, stop=True),
                               waits=[ld_tgt, pw], incs=[T1.bump(1)])
                            em("vector", lambda E, psx=psx, c=c, t=t: E.tensor_scalar_mul(
                                x2[:, c * 1024 + t * 512:c * 1024 + (t + 1) * 512],
                                psx, R64),
                               waits=[(T1.h, T1.v),
                                      (MM.h, down_done) if (tw > 0 and c == 0 and t == 0) else None],
                               incs=[EV2.bump(1)])
                            pool2.release(pi)
                        lrot.release(li)
                    x2_tgt = EV2.v
                    if tw > 0:
                        emit_final(2 * (tw - 1))
                        emit_final(2 * (tw - 1) + 1)
                    for o2c in range(16):
                        wi, wslab, ww = wdrot.acquire()
                        for q in range(4):
                            em("gpsimd", lambda E, wslab=wslab, o2c=o2c, q=q: E.dma_start(
                                out=wslab[:, q * 2048:(q + 1) * 2048].rearrange(
                                    "p (c o) -> p c o", c=16),
                                in_=wd_t[o2c, q * 16:(q + 1) * 16].transpose([1, 0, 2])),
                               waits=[ww if q == 0 else None],
                               incs=[SD[2 + wi].bump(16)], dma=True)
                        wq_tgt = SD[2 + wi].at()
                        sli, sslot, slw = slrot.acquire()
                        em("scalar", lambda E, d=sslot, o2c=o2c, tw=tw: E.dma_start(
                            out=d, in_=s_scr[o2c * 128:(o2c + 1) * 128,
                                             tw * 1024:(tw + 1) * 1024]),
                           waits=[slw], incs=[SD[4 + sli].bump(16)], dma=True)
                        sl_tgt = SD[4 + sli].at()
                        for th in range(2):
                            pi, psx, pw = dpool.acquire()

                            def dmm(E, wslab=wslab, th=th, psx=psx):
                                for c in range(64):
                                    ins = E.matmul(
                                        psx, wslab[:, c * 128:(c + 1) * 128],
                                        x2[:, c * 1024 + th * 512:c * 1024 + (th + 1) * 512],
                                        start=(c == 0), stop=(c == 63))
                                return ins
                            em("tensor", dmm,
                               waits=[wq_tgt, pw,
                                      (EV2.h, x2_tgt) if (o2c == 0 and th == 0) else None],
                               incs=[T1.bump(1)] + ([MM.bump(1)] if th == 1 else []))
                            d_tgt = T1.v
                            gi2, gslot, gw2 = grot.acquire()
                            em("scalar", lambda E, psx=psx, d=gslot, o2c=o2c:
                               E.activation(d, psx, AF.Silu,
                                            scale=wsdp_sb[:, o2c:o2c + 1]),
                               waits=[(T1.h, d_tgt), gw2], incs=[EV.bump(1)])
                            dpool.release(pi)
                            ri, rslot, rw2 = rrot.acquire()
                            em("vector", lambda E, d=rslot, g=gslot, ss=sslot, th=th:
                               E.tensor_mul(d, g, ss[:, th * 512:(th + 1) * 512]),
                               waits=[(EV.h, EV.v), sl_tgt, rw2],
                               incs=[V1.bump(1)])
                            grot.release(gi2)
                            if th == 1:
                                slrot.release(sli)
                            em("sync", lambda E, s=rslot, o2c=o2c, tw=tw, th=th:
                               E.dma_start(out=r_scr[o2c, tw * 2 + th], in_=s),
                               waits=[(V1.h, V1.v)], incs=[rrot.sems[ri].bump(16)], dma=True)
                            rs_val[(o2c, tw * 2 + th)] = rrot.sems[ri].at()
                            rrot.release(ri)
                        wdrot.release(wi)
                    down_done = MM.v
                emit_final(6)
                emit_final(7)
                em("gpsimd", lambda E: E.engine_nop(), waits=[(QS.h, QS.v)])
                em.gate(PH)
                em.flush(nc)

    return nc


def _prep(inputs):
    x = np.ascontiguousarray(np.asarray(inputs["L0_attn_hidden_states"], dtype=np.float32))
    w_qkv = np.asarray(inputs["L0_attn_w_qkv"], dtype=np.float32)
    w_o = np.asarray(inputs["L0_attn_w_o"], dtype=np.float32)
    su_ug = np.asarray(inputs["L0_ffn_SU_ug"], dtype=np.float32)
    su_dp = np.asarray(inputs["L0_ffn_SU_dp"], dtype=np.float32)
    ws_ug = np.asarray(inputs["L0_ffn_Wscale_ug"], dtype=np.float32)
    ws_dp = np.asarray(inputs["L0_ffn_Wscale_dp"], dtype=np.float32)
    W_up = np.asarray(inputs["L0_ffn_W_up"], dtype=np.float32)
    W_gate = np.asarray(inputs["L0_ffn_W_gate"], dtype=np.float32)
    W_down = np.asarray(inputs["L0_ffn_W_down"], dtype=np.float32)

    cm = np.ascontiguousarray
    d = {}
    d["wqkv_t"] = cm(w_qkv.T.reshape(16, 128, 24, 256).transpose(2, 0, 1, 3))
    d["wo_t"] = cm(w_o.T.reshape(16, 128, 4, 512).transpose(2, 0, 1, 3))
    d["wup_t"] = cm(W_up.T.reshape(128, 16, 8192).transpose(1, 0, 2)
                    .reshape(16, 128, 32, 256).transpose(2, 0, 1, 3))
    d["wgate_t"] = cm(W_gate.T.reshape(128, 16, 8192).transpose(1, 0, 2)
                      .reshape(16, 128, 32, 256).transpose(2, 0, 1, 3))
    d["wd_t"] = cm(W_down.T.reshape(128, 64, 2048).transpose(1, 0, 2)
                   .reshape(64, 128, 16, 128).transpose(2, 0, 1, 3))
    d["su_ug_t"] = cm(su_ug.reshape(16, 128).T)
    d["su_dp_t"] = cm(su_dp.reshape(64, 128).T)
    d["ws_up_t"] = cm(ws_ug[:INTER].reshape(64, 128).T)
    d["ws_gate_t"] = cm(ws_ug[INTER:].reshape(64, 128).T)
    d["ws_dp_t"] = cm(ws_dp.reshape(16, 128).T)
    d["h128"] = _hadamard(128)
    d["h16i8"] = cm(np.kron(_hadamard(16), np.eye(8, dtype=np.float32)))
    d["h64i2"] = cm(np.kron(_hadamard(64), np.eye(2, dtype=np.float32)))
    d["ident"] = np.eye(128, dtype=np.float32)
    d["ones"] = np.ones((128, 128), dtype=np.float32)
    cv = np.zeros((128, 8), dtype=np.float32)
    for i, v in enumerate((EPS_LN, -0.5, EPS_SM, SQRT_C)):
        cv[:, i] = v
    d["cvec"] = cv
    kp = np.arange(128)[:, None]
    qo = np.arange(512)[None, :]
    d["masks"] = np.stack(
        [np.where(dd * 128 + kp <= qo, 1.0, 0.0) for dd in range(4)]
    ).astype(np.float32)
    return x, d


def kernel(**inputs):
    x, consts = _prep(inputs)
    nc = build()
    in_maps = []
    for b in range(NCORES):
        m = dict(consts)
        m["x"] = x[b]
        in_maps.append(m)
    res = run_bass_kernel_spmd(nc, in_maps, core_ids=list(range(NCORES)))
    return np.stack([np.asarray(r["out"], dtype=np.float32) for r in res.results], axis=0)
